# version 11
# baseline (speedup 1.0000x reference)
import numpy as np
from contextlib import ExitStack

import concourse.bass as bass
import concourse.mybir as mybir
from concourse.bass_utils import run_bass_kernel_spmd

F32 = mybir.dt.float32
AF = mybir.ActivationFunctionType
OP = mybir.AluOpType

B_LOC = 2
S = 2048
D = 1024
TOK = 256
NTILE = B_LOC * S // TOK
ENGS = ("sync", "gpsimd", "vector", "scalar", "tensor")


def build_nc():
    nc = bass.Bass("TRN2", target_bir_lowering=False)
    x_h = nc.dram_tensor("x", [B_LOC, S, D], F32, kind="ExternalInput").ap()
    o_h = nc.dram_tensor("o", [B_LOC, S, D], F32, kind="ExternalOutput").ap()
    winT_h = nc.dram_tensor("winT", [128, 8, 256], F32, kind="ExternalInput").ap()
    woutT_h = nc.dram_tensor("woutT", [128, 2, 1024], F32, kind="ExternalInput").ap()
    wskipT_h = nc.dram_tensor("wskipT", [128, 8, 1024], F32, kind="ExternalInput").ap()
    u1_h = nc.dram_tensor("u1", [128, 8, 256], F32, kind="ExternalInput").ap()
    v1_h = nc.dram_tensor("v1", [128, 2, 4096], F32, kind="ExternalInput").ap()
    u2_h = nc.dram_tensor("u2", [128, 32, 256], F32, kind="ExternalInput").ap()
    v2_h = nc.dram_tensor("v2", [128, 2, 1024], F32, kind="ExternalInput").ap()
    nwf_h = nc.dram_tensor("nwf", [128, 8], F32, kind="ExternalInput").ap()
    bo_h = nc.dram_tensor("bo", [128, 1024], F32, kind="ExternalInput").ap()
    bs_h = nc.dram_tensor("bs", [128, 1024], F32, kind="ExternalInput").ap()
    b2_h = nc.dram_tensor("b2", [128, 1024], F32, kind="ExternalInput").ap()
    bin_h = nc.dram_tensor("bin", [128, 2], F32, kind="ExternalInput").ap()
    b1_h = nc.dram_tensor("b1", [128, 32], F32, kind="ExternalInput").ap()
    ldec_h = nc.dram_tensor("ldec", [128, 2], F32, kind="ExternalInput").ap()
    eye_h = nc.dram_tensor("eye", [128, 128], F32, kind="ExternalInput").ap()

    es = ExitStack()
    sb = lambda name, shape: es.enter_context(nc.sbuf_tensor(name, shape, F32))
    WIN = sb("WIN", [128, 8, 256])
    WOUT = sb("WOUT", [128, 2, 1024])
    WSKIP = sb("WSKIP", [128, 8, 1024])
    V1 = sb("V1", [128, 2, 4096])
    U2 = sb("U2", [128, 32, 256])
    V2 = sb("V2", [128, 2, 1024])
    BADD = sb("BADD", [128, 1024])
    B2R = sb("B2R", [128, 1024])
    BIN = sb("BIN", [128, 2])
    B1 = sb("B1", [128, 32])
    LDEC = sb("LDEC", [128, 2])
    NWF = sb("NWF", [128, 8])
    DECS = sb("DECS", [128, 2])
    EYE = sb("EYE", [128, 128])
    U = sb("U", [128, 2, 2 * S])
    XA = sb("XA", [128, 2048])
    SCR = sb("SCR", [128, 2048])
    T0 = sb("T0", [128, 2048])
    ATT = sb("ATT", [128, 2048])
    FFNB = sb("FFNB", [128, 2048])
    XNT = sb("XNT", [128, 4, 256])
    ST6 = sb("ST6", [128, 4, 6])
    ST2 = sb("ST2", [128, 2, 2])
    MSQ = sb("MSQ", [128, 2])
    MS = sb("MS", [128, 2])
    SD = sb("SD", [128, 2])
    RR = sb("RR", [128, 2])
    ZERO = sb("ZERO", [128, 1])
    EPS5 = sb("EPS5", [128, 1])
    EPS4 = sb("EPS4", [128, 1])
    PADT = sb("PADT", [128, 256])
    PADS = sb("PADS", [128, 256])

    ps = lambda name, shape: es.enter_context(nc.psum_tensor(name, shape, F32))
    PT0 = ps("PT0", [128, 4, 128])
    PT1 = ps("PT1", [128, 4, 128])
    PA = ps("PA", [128, 512])
    PB = ps("PB", [128, 512])
    PU = ps("PU", [128, 512])
    PP = ps("PP", [128, 512])
    PH = ps("PH", [128, 2, 256])
    PQ = ps("PQ", [128, 2, 256])

    sem_names = ("LX", "LW", "LW2", "VS", "SS", "TS", "HS", "OS")
    semh = {s: es.enter_context(nc.semaphore(s)) for s in sem_names}

    ops = {e: [] for e in ENGS}
    cnt = {s: 0 for s in sem_names}

    def w(eng, sem, val):
        if val > 0:
            ops[eng].append(lambda eh, sm=sem, v=val: eh.wait_ge(semh[sm], v))

    def e(eng, fn, inc=None, k=1):
        if inc is None:
            ops[eng].append(fn)
            return None
        cnt[inc] += k
        ops[eng].append(lambda eh, f=fn, s=inc, kk=k: f(eh).then_inc(semh[s], kk))
        return cnt[inc]

    def vpad(inc=None):
        e("vector", lambda eh: eh.tensor_copy(PADT[:, 0:128], PADT[:, 128:256]))
        e("vector", lambda eh: eh.tensor_copy(PADT[:, 128:256], PADT[:, 0:128]))
        return e("vector", lambda eh: eh.tensor_copy(PADT[:, 0:128], PADT[:, 128:256]), inc=inc)

    def spad(inc=None):
        e("scalar", lambda eh: eh.activation(PADS[:, 0:128], PADS[:, 128:256], AF.Copy))
        e("scalar", lambda eh: eh.activation(PADS[:, 128:256], PADS[:, 0:128], AF.Copy))
        return e("scalar", lambda eh: eh.activation(PADS[:, 0:128], PADS[:, 128:256], AF.Copy), inc=inc)

    wcnt = {}
    for name, dst, src in (
        ("eye", EYE[:], eye_h[:]), ("nw", NWF[:], nwf_h[:]), ("bin", BIN[:], bin_h[:]),
        ("b1", B1[:], b1_h[:]), ("ldec", LDEC[:], ldec_h[:]), ("bo", BADD[:], bo_h[:]),
        ("bs", SCR[:, 0:1024], bs_h[:]), ("b2", B2R[:], b2_h[:]),
        ("winT", WIN[:], winT_h[:]), ("woutT", WOUT[:], woutT_h[:]),
        ("wskipT", WSKIP[:], wskipT_h[:]),
    ):
        wcnt[name] = e("sync", lambda eh, d=dst, s=src: eh.dma_start(out=d, in_=s), inc="LW", k=16)
    w2cnt = {}
    for name, dst, src in (("v1", V1[:], v1_h[:]), ("u2", U2[:], u2_h[:]),
                           ("v2", V2[:], v2_h[:])):
        w2cnt[name] = e("scalar", lambda eh, d=dst, s=src: eh.dma_start(out=d, in_=s), inc="LW2", k=16)

    w("scalar", "LW", wcnt["ldec"])
    e("scalar", lambda eh: eh.activation(DECS[:], LDEC[:], AF.Sigmoid))
    ss_decs = spad(inc="SS")

    e("vector", lambda eh: eh.memset(ZERO[:], 0.0))
    e("vector", lambda eh: eh.memset(EPS5[:], 1e-5))
    e("vector", lambda eh: eh.memset(EPS4[:], 1e-4))
    e("vector", lambda eh: eh.memset(PADT[:], 0.0))
    e("vector", lambda eh: eh.memset(PADS[:], 0.0))
    w("vector", "LW", wcnt["bs"])
    e("vector", lambda eh: eh.tensor_tensor(BADD[:], BADD[:], SCR[:, 0:1024], OP.add))
    vs_badd = vpad(inc="VS")
    w("vector", "LW", wcnt["winT"])
    for c in range(8):
        e("vector", lambda eh, cc=c: eh.tensor_scalar_mul(
            WIN[:, cc, :], WIN[:, cc, :], NWF[:, cc:cc + 1]))
    vs_winfold = vpad(inc="VS")

    XB = [XA, SCR]
    xfree = [0] * (2 * NTILE + 2)

    def emit_xload(slot, vs_guard):
        i = slot % NTILE
        b, t = i // 8, (i % 8) * TOK
        buf = XB[slot % 2]
        if vs_guard:
            w("gpsimd", "VS", vs_guard)
        for blk in range(2):
            e("gpsimd", lambda eh, bb=b, tt=t + blk * 128, dst=buf[:, blk * 1024:(blk + 1) * 1024]:
              eh.dma_start(out=dst, in_=x_h[bb, tt:tt + 128, :]), inc="LX", k=16)

    def ln_stats(src):
        for blk in range(2):
            for g in range(2):
                e("vector", lambda eh, b=blk, gg=g, s=src:
                  eh.bn_stats(ST6[:, 2 * b + gg, :],
                              s[:, b * 1024 + gg * 512: b * 1024 + (gg + 1) * 512]))
        for blk in range(2):
            e("vector", lambda eh, b=blk: eh.bn_aggr(ST2[:, b, :], ST6[:, 2 * b:2 * b + 2, :]))

    def sqrt_recip(src_col_fn, eps):
        for blk in range(2):
            e("scalar", lambda eh, b=blk: eh.activation(
                SD[:, b:b + 1], src_col_fn(b), AF.Sqrt, bias=eps[:, 0:1]))
        ss = spad(inc="SS")
        w("vector", "SS", ss)
        for blk in range(2):
            e("vector", lambda eh, b=blk: eh.reciprocal(RR[:, b:b + 1], SD[:, b:b + 1]))
        e("vector", lambda eh: eh.tensor_copy(PADT[:, 1:2], ZERO[:]))

    def emit_tp_group(src, g):
        for blk in range(2):
            PT = (PT0, PT1)[blk]
            for j in range(4):
                c = 4 * g + j
                e("tensor", lambda eh, bb=blk, cc=c, jj=j, P=PT, s=src: eh.transpose(
                    P[:, jj, :], s[:, bb * 1024 + cc * 128: bb * 1024 + (cc + 1) * 128], EYE[:]),
                  inc="TS" if j == 3 else None)
            w("vector", "TS", cnt["TS"])
            e("vector", lambda eh, bb=blk, P=PT: eh.tensor_copy(
                XNT[:, :, bb * 128:(bb + 1) * 128], P[:, :, :]))
            ev = vpad(inc="VS")
            w("tensor", "VS", ev)

    def vfront(slot, xn_dst, tp_guard_ts=0):
        buf = XB[slot % 2]
        w("vector", "LX", 32 * (slot + 1))
        vs_st = None
        ln_stats(buf)
        vs_st = vpad(inc="VS")
        w("scalar", "VS", vs_st)
        sqrt_recip(lambda blk: ST2[:, blk, 1:2], EPS5)
        for blk in range(2):
            e("vector", lambda eh, bb=blk, s=buf: eh.tensor_scalar(
                T0[:, bb * 1024:(bb + 1) * 1024], s[:, bb * 1024:(bb + 1) * 1024],
                ST2[:, bb, 0:1], RR[:, bb:bb + 1], OP.subtract, OP.mult))
        ln_stats(T0)
        for blk in range(2):
            e("vector", lambda eh, bb=blk: eh.tensor_tensor(
                MSQ[:, bb:bb + 1], ST2[:, bb, 0:1], ST2[:, bb, 0:1], OP.mult))
        for blk in range(2):
            e("vector", lambda eh, bb=blk: eh.tensor_tensor(
                MS[:, bb:bb + 1], MSQ[:, bb:bb + 1], ST2[:, bb, 1:2], OP.add))
        vs_st2 = vpad(inc="VS")
        w("scalar", "VS", vs_st2)
        sqrt_recip(lambda blk: MS[:, blk:blk + 1], EPS4)
        if tp_guard_ts:
            w("vector", "TS", tp_guard_ts)
        for blk in range(2):
            e("vector", lambda eh, bb=blk, d=xn_dst: eh.tensor_scalar_mul(
                d[:, bb * 1024:(bb + 1) * 1024], T0[:, bb * 1024:(bb + 1) * 1024],
                RR[:, bb:bb + 1]))
        return vpad(inc="VS")

    vs_uevac = [0] * NTILE
    vs_xn1 = [0] * NTILE
    ts_tp1 = [0] * NTILE

    emit_xload(0, 0)
    emit_xload(1, vs_badd)
    xnb = lambda i: (ATT, FFNB)[i % 2]
    vs_xn1[0] = vfront(0, xnb(0))
    xfree[0] = vs_xn1[0]
    emit_xload(2, xfree[0])
    vs_xn1[1] = vfront(1, xnb(1))
    xfree[1] = vs_xn1[1]
    emit_xload(3, xfree[1])

    for i in range(NTILE):
        b, t = i // 8, (i % 8) * TOK
        if i == 0:
            w("tensor", "VS", vs_winfold)
        w("tensor", "VS", vs_xn1[i])
        if i >= 1:
            w("tensor", "VS", vs_uevac[i - 1])
        for g in range(2):
            emit_tp_group(xnb(i), g)
            for nt in range(2):
                PO = (PU, PA)[nt]
                for c in range(4 * g, 4 * g + 4):
                    e("tensor", lambda eh, n=nt, cc=c, P=PO: eh.matmul(
                        P[:, 0:256], WIN[:, cc, n * 128:(n + 1) * 128],
                        XNT[:, cc % 4, :], start=(cc == 0), stop=(cc == 7)),
                      inc="TS" if (c == 7 and nt == 1) else None)
        ts_tp1[i] = cnt["TS"]
        w("vector", "TS", ts_tp1[i])
        if i == 0:
            w("vector", "LW", wcnt["bin"])
        for nt in range(2):
            PO = (PU, PA)[nt]
            e("vector", lambda eh, n=nt, bb=b, tt=t, P=PO: eh.tensor_scalar_add(
                U[:, n, bb * S + tt: bb * S + tt + TOK], P[:, 0:256],
                BIN[:, n:n + 1]))
        vs_uevac[i] = vpad(inc="VS")
        if i + 2 < NTILE:
            vs_xn1[i + 2] = vfront(i + 2, xnb(i + 2), tp_guard_ts=ts_tp1[i])
            xfree[i + 2] = vs_xn1[i + 2]
            if i + 4 < NTILE:
                emit_xload(i + 4, xfree[i + 2])

    w("vector", "LW", wcnt["wskipT"])
    for c in range(8):
        e("vector", lambda eh, cc=c: eh.tensor_scalar_mul(
            WSKIP[:, cc, :], WSKIP[:, cc, :], NWF[:, cc:cc + 1]))
    vpad()
    w("scalar", "TS", ts_tp1[NTILE - 1])
    w2cnt["u1"] = e("scalar", lambda eh: eh.dma_start(out=WIN[:], in_=u1_h[:]), inc="LW2", k=16)

    w("vector", "SS", ss_decs)
    for nt in range(2):
        e("vector", lambda eh: eh.memset(SCR[:], 1.0))
        vpad()
        e("vector", lambda eh, n=nt: eh.tensor_scalar_mul(SCR[:], SCR[:], DECS[:, n:n + 1]))
        vpad()
        for bb in range(2):
            e("vector", lambda eh, n=nt, b2=bb: eh.tensor_tensor_scan(
                U[:, n, b2 * S:(b2 + 1) * S], SCR[:], U[:, n, b2 * S:(b2 + 1) * S],
                ZERO[:, 0:1], OP.mult, OP.add))
    vs_scan = vpad(inc="VS")
    w("tensor", "VS", vs_scan)
    emit_xload(NTILE + 0, xfree[14])

    os_done = [0] * NTILE

    def vfront2(i):
        slot = NTILE + i
        buf = XB[i % 2]
        w("vector", "LX", 32 * (slot + 1))
        ln_stats(buf)
        vs_st = vpad(inc="VS")
        w("scalar", "VS", vs_st)
        sqrt_recip(lambda blk: ST2[:, blk, 1:2], EPS5)
        for blk in range(2):
            e("vector", lambda eh, bb=blk, s=buf: eh.tensor_scalar(
                T0[:, bb * 1024:(bb + 1) * 1024], s[:, bb * 1024:(bb + 1) * 1024],
                ST2[:, bb, 0:1], RR[:, bb:bb + 1], OP.subtract, OP.mult))
        ln_stats(T0)
        for blk in range(2):
            e("vector", lambda eh, bb=blk: eh.tensor_tensor(
                MSQ[:, bb:bb + 1], ST2[:, bb, 0:1], ST2[:, bb, 0:1], OP.mult))
        for blk in range(2):
            e("vector", lambda eh, bb=blk: eh.tensor_tensor(
                MS[:, bb:bb + 1], MSQ[:, bb:bb + 1], ST2[:, bb, 1:2], OP.add))
        vs_st2 = vpad(inc="VS")
        w("scalar", "VS", vs_st2)
        sqrt_recip(lambda blk: MS[:, blk:blk + 1], EPS4)
        for blk in range(2):
            e("vector", lambda eh, bb=blk: eh.tensor_scalar_mul(
                ATT[:, bb * 1024:(bb + 1) * 1024], T0[:, bb * 1024:(bb + 1) * 1024],
                RR[:, bb:bb + 1]))
        return vpad(inc="VS")

    vs_xn2 = [0] * NTILE
    vs_xn2[0] = vfront2(0)
    emit_xload(NTILE + 1, vs_scan)

    COMBO = [(0, 0, PA), (0, 1, PB), (1, 0, PU), (1, 1, PP)]
    for i in range(NTILE):
        b, t = i // 8, (i % 8) * TOK
        buf = XB[i % 2]
        if 1 <= i and i + 1 < NTILE:
            emit_xload(NTILE + i + 1, xfree[NTILE + i - 1])

        w("tensor", "VS", vs_xn2[i])
        for g in range(2):
            emit_tp_group(ATT, g)
            if g == 0:
                for blk, half, PO in COMBO:
                    for nt in range(2):
                        e("tensor", lambda eh, n=nt, bb=b, ttk=t + blk * 128, hh=half, P=PO: eh.matmul(
                            P[:], U[:, n, bb * S + ttk: bb * S + ttk + 128],
                            WOUT[:, n, hh * 512:(hh + 1) * 512],
                            start=(n == 0), stop=False))
            for blk, half, PO in COMBO:
                for c in range(4 * g, 4 * g + 4):
                    e("tensor", lambda eh, cc=c, bb=blk, hh=half, P=PO: eh.matmul(
                        P[:], XNT[:, cc % 4, bb * 128:(bb + 1) * 128],
                        WSKIP[:, cc, hh * 512:(hh + 1) * 512],
                        start=False, stop=(cc == 7)),
                      inc="TS" if (c == 7 and blk == 1 and half == 1) else None)
        ts_y = cnt["TS"]

        w("vector", "TS", ts_y)
        if i == 0:
            w("vector", "LW", wcnt["b2"])
        for blk, half, PO in COMBO:
            e("vector", lambda eh, bb=blk, hh=half, P=PO: eh.tensor_tensor(
                ATT[:, bb * 1024 + hh * 512: bb * 1024 + (hh + 1) * 512], P[:],
                T0[:, bb * 1024 + hh * 512: bb * 1024 + (hh + 1) * 512], OP.add))
        for blk, half, PO in COMBO:
            e("vector", lambda eh, bb=blk, hh=half: eh.tensor_tensor(
                ATT[:, bb * 1024 + hh * 512: bb * 1024 + (hh + 1) * 512],
                ATT[:, bb * 1024 + hh * 512: bb * 1024 + (hh + 1) * 512],
                BADD[:, hh * 512:(hh + 1) * 512], OP.add))
        ev = vpad(inc="VS")
        w("tensor", "VS", ev)
        ln_stats(ATT)
        vs_st = vpad(inc="VS")
        w("scalar", "VS", vs_st)
        sqrt_recip(lambda blk: ST2[:, blk, 1:2], EPS5)
        for blk in range(2):
            e("vector", lambda eh, bb=blk: eh.tensor_scalar(
                T0[:, bb * 1024:(bb + 1) * 1024], ATT[:, bb * 1024:(bb + 1) * 1024],
                ST2[:, bb, 0:1], RR[:, bb:bb + 1], OP.subtract, OP.mult))
        for blk in range(2):
            e("vector", lambda eh, bb=blk, s=buf: eh.tensor_tensor(
                s[:, bb * 1024:(bb + 1) * 1024], s[:, bb * 1024:(bb + 1) * 1024],
                T0[:, bb * 1024:(bb + 1) * 1024], OP.add))
        ln_stats(buf)
        vs_st = vpad(inc="VS")
        w("scalar", "VS", vs_st)
        sqrt_recip(lambda blk: ST2[:, blk, 1:2], EPS5)
        for blk in range(2):
            e("vector", lambda eh, bb=blk, s=buf: eh.tensor_scalar(
                T0[:, bb * 1024:(bb + 1) * 1024], s[:, bb * 1024:(bb + 1) * 1024],
                ST2[:, bb, 0:1], RR[:, bb:bb + 1], OP.subtract, OP.mult))
        vs_t1 = vpad(inc="VS")

        w("tensor", "VS", vs_t1)
        if i == 0:
            w("tensor", "LW2", w2cnt["u1"])
        for g in range(2):
            emit_tp_group(T0, g)
            for r in range(2):
                PO = (PU, PA)[r]
                for c in range(4 * g, 4 * g + 4):
                    e("tensor", lambda eh, rr=r, cc=c, P=PO: eh.matmul(
                        P[:, 0:256], WIN[:, cc, rr * 128:(rr + 1) * 128],
                        XNT[:, cc % 4, :], start=(cc == 0), stop=(cc == 7)),
                      inc="TS" if (c == 7 and r == 1) else None)
        w("vector", "TS", cnt["TS"])
        if i >= 1:
            w("vector", "OS", os_done[i - 1])
        for r in range(2):
            PO = (PU, PA)[r]
            e("vector", lambda eh, rr=r, P=PO: eh.tensor_copy(
                FFNB[:, rr * 256:(rr + 1) * 256], P[:, 0:256]))
        vs_p = vpad(inc="VS")
        w("tensor", "VS", vs_p)

        hs_base = cnt["HS"]
        for c in range(33):
            if c < 32:
                if c >= 2:
                    w("tensor", "HS", hs_base + c - 1)
                for r in range(2):
                    e("tensor", lambda eh, rr=r, cc=c: eh.matmul(
                        PH[:, cc % 2, :], V1[:, rr, cc * 128:(cc + 1) * 128],
                        FFNB[:, rr * 256:(rr + 1) * 256],
                        start=(rr == 0), stop=(rr == 1)),
                      inc="TS" if r == 1 else None)
                w("scalar", "TS", cnt["TS"])
                if i == 0 and c == 0:
                    w("scalar", "LW", wcnt["b1"])
                e("scalar", lambda eh, cc=c: eh.activation(
                    FFNB[:, 1536 + (cc % 2) * 256: 1536 + (cc % 2 + 1) * 256],
                    PH[:, cc % 2, :], AF.Gelu_apprx_tanh, bias=B1[:, cc:cc + 1]))
                spad(inc="HS")
            if c >= 1:
                cc0 = c - 1
                w("tensor", "HS", hs_base + cc0 + 1)
                for r in range(2):
                    PO = PQ[:, 0, :] if r == 0 else PB[:, 0:256]
                    e("tensor", lambda eh, rr=r, cc=cc0, P=PO: eh.matmul(
                        P, U2[:, cc, rr * 128:(rr + 1) * 128],
                        FFNB[:, 1536 + (cc % 2) * 256: 1536 + (cc % 2 + 1) * 256],
                        start=(cc == 0), stop=(cc == 31)),
                      inc="TS" if (r == 1 and cc0 == 31) else None)
        ts_q = cnt["TS"]

        if i + 1 < NTILE:
            vs_xn2[i + 1] = vfront2(i + 1)

        w("vector", "TS", ts_q)
        for r in range(2):
            PO = PQ[:, 0, :] if r == 0 else PB[:, 0:256]
            e("vector", lambda eh, rr=r, P=PO: eh.tensor_copy(
                FFNB[:, 1024 + rr * 256:1024 + (rr + 1) * 256], P))
        vs_q = vpad(inc="VS")
        w("tensor", "VS", vs_q)
        for blk in range(2):
            for half in range(2):
                PO = (PA, PB)[half]
                for r in range(2):
                    e("tensor", lambda eh, rr=r, bb=blk, hh=half, P=PO: eh.matmul(
                        P[:], FFNB[:, 1024 + rr * 256 + bb * 128: 1024 + rr * 256 + (bb + 1) * 128],
                        V2[:, rr, hh * 512:(hh + 1) * 512],
                        start=(rr == 0), stop=(rr == 1)),
                      inc="TS" if r == 1 else None)
            w("vector", "TS", cnt["TS"])
            for half in range(2):
                PO = (PA, PB)[half]
                e("vector", lambda eh, bb=blk, hh=half, P=PO: eh.tensor_tensor(
                    FFNB[:, bb * 1024 + hh * 512: bb * 1024 + (hh + 1) * 512], P[:],
                    B2R[:, hh * 512:(hh + 1) * 512], OP.add))
            evb = vpad(inc="VS")
            w("tensor", "VS", evb)
        ln_stats(FFNB)
        vs_st = vpad(inc="VS")
        w("scalar", "VS", vs_st)
        sqrt_recip(lambda blk: ST2[:, blk, 1:2], EPS5)
        for blk in range(2):
            e("vector", lambda eh, bb=blk: eh.tensor_scalar(
                FFNB[:, bb * 1024:(bb + 1) * 1024], FFNB[:, bb * 1024:(bb + 1) * 1024],
                ST2[:, bb, 0:1], RR[:, bb:bb + 1], OP.subtract, OP.mult))
        for blk in range(2):
            e("vector", lambda eh, bb=blk, s=buf: eh.tensor_tensor(
                FFNB[:, bb * 1024:(bb + 1) * 1024], FFNB[:, bb * 1024:(bb + 1) * 1024],
                s[:, bb * 1024:(bb + 1) * 1024], OP.add))
        evf = vpad(inc="VS")
        xfree[NTILE + i] = evf
        w("gpsimd", "VS", evf)
        for blk in range(2):
            e("gpsimd", lambda eh, bb=b, ttk=t + blk * 128, o=blk: eh.dma_start(
                out=o_h[bb, ttk:ttk + 128, :], in_=FFNB[:, o * 1024:(o + 1) * 1024]),
              inc="OS", k=16)
        os_done[i] = cnt["OS"]

    w("gpsimd", "OS", cnt["OS"])

    with es, nc.Block() as block:
        @block.sync
        def _(sync):
            for f in ops["sync"]:
                f(sync)

        @block.gpsimd
        def _(gpsimd):
            for f in ops["gpsimd"]:
                f(gpsimd)

        @block.vector
        def _(vector):
            for f in ops["vector"]:
                f(vector)

        @block.scalar
        def _(scalar):
            for f in ops["scalar"]:
                f(scalar)

        @block.tensor
        def _(tensor):
            for f in ops["tensor"]:
                f(tensor)
    return nc


def kernel(L0_attn_tokens, L0_attn_norm_w, L0_attn_W_in, L0_attn_b_in,
           L0_attn_W_out, L0_attn_b_out, L0_attn_W_skip, L0_attn_b_skip,
           L0_attn_log_decay, L0_ffn_U1, L0_ffn_V1, L0_ffn_b1, L0_ffn_U2,
           L0_ffn_V2, L0_ffn_b2):
    f = np.float32
    tok = np.ascontiguousarray(np.asarray(L0_attn_tokens, f))
    rep = lambda v: np.ascontiguousarray(np.broadcast_to(np.asarray(v, f), (128, 1024)))
    chunked = lambda wmat, nch, cols: np.ascontiguousarray(
        np.asarray(wmat, f).reshape(nch, 128, cols).transpose(1, 0, 2))
    com = {
        "winT": chunked(np.asarray(L0_attn_W_in, f).T, 8, 256),
        "woutT": chunked(np.asarray(L0_attn_W_out, f).T, 2, 1024),
        "wskipT": chunked(np.asarray(L0_attn_W_skip, f).T, 8, 1024),
        "u1": chunked(L0_ffn_U1, 8, 256),
        "v1": chunked(L0_ffn_V1, 2, 4096),
        "u2": chunked(L0_ffn_U2, 32, 256),
        "v2": chunked(L0_ffn_V2, 2, 1024),
        "nwf": np.ascontiguousarray(np.asarray(L0_attn_norm_w, f).reshape(8, 128).T),
        "bo": rep(L0_attn_b_out),
        "bs": rep(L0_attn_b_skip),
        "b2": rep(L0_ffn_b2),
        "bin": np.ascontiguousarray(np.asarray(L0_attn_b_in, f).reshape(2, 128).T),
        "b1": np.ascontiguousarray(np.asarray(L0_ffn_b1, f).reshape(32, 128).T),
        "ldec": np.ascontiguousarray(np.asarray(L0_attn_log_decay, f).reshape(2, 128).T),
        "eye": np.eye(128, dtype=f),
    }
    in_maps = []
    for c in range(8):
        m = dict(com)
        m["x"] = np.ascontiguousarray(tok[2 * c:2 * c + 2])
        in_maps.append(m)
    nc = build_nc()
    res = run_bass_kernel_spmd(nc, in_maps, core_ids=list(range(8)))
    out = np.concatenate([res.results[c]["o"] for c in range(8)], axis=0)
    return np.ascontiguousarray(out.astype(np.float32))


# version 14
# speedup vs baseline: 1.1143x; 1.1143x over previous
import numpy as np
from contextlib import ExitStack

import concourse.bass as bass
import concourse.mybir as mybir
from concourse.bass_utils import run_bass_kernel_spmd

F32 = mybir.dt.float32
AF = mybir.ActivationFunctionType
OP = mybir.AluOpType

B_LOC = 2
S = 2048
D = 1024
TOK = 256
NTILE = B_LOC * S // TOK
ENGS = ("sync", "gpsimd", "vector", "scalar", "tensor")


def build_nc():
    nc = bass.Bass("TRN2", target_bir_lowering=False)
    x_h = nc.dram_tensor("x", [B_LOC, S, D], F32, kind="ExternalInput").ap()
    o_h = nc.dram_tensor("o", [B_LOC, S, D], F32, kind="ExternalOutput").ap()
    winT_h = nc.dram_tensor("winT", [128, 8, 256], F32, kind="ExternalInput").ap()
    woutT_h = nc.dram_tensor("woutT", [128, 2, 1024], F32, kind="ExternalInput").ap()
    wskipT_h = nc.dram_tensor("wskipT", [128, 8, 1024], F32, kind="ExternalInput").ap()
    u1_h = nc.dram_tensor("u1", [128, 8, 256], F32, kind="ExternalInput").ap()
    v1_h = nc.dram_tensor("v1", [128, 2, 4096], F32, kind="ExternalInput").ap()
    u2_h = nc.dram_tensor("u2", [128, 32, 256], F32, kind="ExternalInput").ap()
    v2_h = nc.dram_tensor("v2", [128, 2, 1024], F32, kind="ExternalInput").ap()
    nwf_h = nc.dram_tensor("nwf", [128, 8], F32, kind="ExternalInput").ap()
    bo_h = nc.dram_tensor("bo", [128, 1024], F32, kind="ExternalInput").ap()
    bs_h = nc.dram_tensor("bs", [128, 1024], F32, kind="ExternalInput").ap()
    b2_h = nc.dram_tensor("b2", [128, 1024], F32, kind="ExternalInput").ap()
    bin_h = nc.dram_tensor("bin", [128, 2], F32, kind="ExternalInput").ap()
    b1_h = nc.dram_tensor("b1", [128, 32], F32, kind="ExternalInput").ap()
    ldec_h = nc.dram_tensor("ldec", [128, 2], F32, kind="ExternalInput").ap()
    eye_h = nc.dram_tensor("eye", [128, 128], F32, kind="ExternalInput").ap()

    es = ExitStack()
    sb = lambda name, shape: es.enter_context(nc.sbuf_tensor(name, shape, F32))
    WIN = sb("WIN", [128, 8, 256])
    WOUT = sb("WOUT", [128, 2, 1024])
    WSKIP = sb("WSKIP", [128, 8, 1024])
    U1 = sb("U1", [128, 8, 256])
    V1 = sb("V1", [128, 2, 4096])
    U2 = sb("U2", [128, 32, 256])
    V2 = sb("V2", [128, 2, 1024])
    BADD = sb("BADD", [128, 1024])
    B2R = sb("B2R", [128, 1024])
    BIN = sb("BIN", [128, 2])
    B1 = sb("B1", [128, 32])
    LDEC = sb("LDEC", [128, 2])
    NWF = sb("NWF", [128, 8])
    DECS = sb("DECS", [128, 2])
    EYE = sb("EYE", [128, 128])
    U = sb("U", [128, 2, 2 * S])
    XA = sb("XA", [128, 2048])
    SCR = sb("SCR", [128, 2048])
    T0 = sb("T0", [128, 2048])
    ATT = sb("ATT", [128, 2048])
    XNT = sb("XNT", [128, 4, 256])
    ST6 = sb("ST6", [128, 4, 6])
    ST2 = sb("ST2", [128, 2, 2])
    MSQ = sb("MSQ", [128, 2])
    MS = sb("MS", [128, 2])
    SD = sb("SD", [128, 2])
    RR = sb("RR", [128, 2])
    ZERO = sb("ZERO", [128, 1])
    EPS5 = sb("EPS5", [128, 1])
    EPS4 = sb("EPS4", [128, 1])
    PADT = sb("PADT", [128, 64])
    PADS = sb("PADS", [128, 64])

    ps = lambda name, shape: es.enter_context(nc.psum_tensor(name, shape, F32))
    PT0 = ps("PT0", [128, 4, 128])
    PT1 = ps("PT1", [128, 4, 128])
    PA = ps("PA", [128, 512])
    PB = ps("PB", [128, 512])
    PU = ps("PU", [128, 512])
    PP = ps("PP", [128, 512])
    PH = ps("PH", [128, 2, 256])
    PQ = ps("PQ", [128, 2, 256])

    sem_names = ("LXA", "LXB", "LWC", "LW", "LW2", "QV", "QS", "QT", "OS")
    semh = {s: es.enter_context(nc.semaphore(s)) for s in sem_names}

    ops = {e: [] for e in ENGS}
    cnt = {s: 0 for s in sem_names}

    def w(eng, sem, val):
        if val > 0:
            ops[eng].append(lambda eh, sm=sem, v=val: eh.wait_ge(semh[sm], v))

    AUTO = {"vector": "QV", "scalar": "QS", "tensor": "QT"}

    def e(eng, fn, inc=None, k=1):
        if inc is None and eng in AUTO:
            inc, k = AUTO[eng], 1
        if inc is None:
            ops[eng].append(fn)
            return None
        cnt[inc] += k
        ops[eng].append(lambda eh, f=fn, s=inc, kk=k: f(eh).then_inc(semh[s], kk))
        return cnt[inc]

    def vb():
        ops["vector"].append(lambda eh, v=cnt["QV"]: eh.wait_ge(semh["QV"], v))

    _pv = [0]

    def vpad(inc=None):
        j1 = _pv[0] % 64
        j2 = (_pv[0] + 1) % 64
        _pv[0] += 2
        e("vector", lambda eh, jj=j1: eh.tensor_copy(PADT[:, jj:jj + 1], ZERO[:]))
        return e("vector", lambda eh, jj=j2: eh.tensor_copy(PADT[:, jj:jj + 1], ZERO[:]))

    _ps = [0]

    def spad(inc=None):
        j = _ps[0] % 64
        _ps[0] += 1
        return e("scalar", lambda eh, jj=j: eh.activation(PADS[:, jj:jj + 1], ZERO[:], AF.Copy))

    for name, dst, src in (
        ("eye", EYE[:], eye_h[:]), ("nw", NWF[:], nwf_h[:]), ("bin", BIN[:], bin_h[:]),
        ("b1", B1[:], b1_h[:]), ("ldec", LDEC[:], ldec_h[:]), ("bo", BADD[:], bo_h[:]),
        ("bs", SCR[:, 0:1024], bs_h[:]), ("b2", B2R[:], b2_h[:]),
        ("winT", WIN[:], winT_h[:]),
    ):
        e("sync", lambda eh, d=dst, s=src: eh.dma_start(out=d, in_=s), inc="LWC", k=16)
    for name, dst, src in (("woutT", WOUT[:], woutT_h[:]), ("wskipT", WSKIP[:], wskipT_h[:])):
        e("sync", lambda eh, d=dst, s=src: eh.dma_start(out=d, in_=s), inc="LW", k=16)
    LWC_ALL, LW_ALL = 144, 32
    wcnt = {k: LWC_ALL for k in ("eye", "nw", "bin", "b1", "ldec", "bo", "bs", "b2", "winT")}
    wcnt.update({"woutT": LW_ALL, "wskipT": LW_ALL})
    w2cnt = {}
    for name, dst, src in (("u1", U1[:], u1_h[:]), ("v1", V1[:], v1_h[:]),
                           ("u2", U2[:], u2_h[:]), ("v2", V2[:], v2_h[:])):
        w2cnt[name] = e("scalar", lambda eh, d=dst, s=src: eh.dma_start(out=d, in_=s), inc="LW2", k=16)

    w("scalar", "LWC", wcnt["ldec"])
    ops["scalar"].append(lambda eh: eh.wait_ge(semh["QV"], 4))
    e("scalar", lambda eh: eh.activation(DECS[:], LDEC[:], AF.Sigmoid))
    ss_decs = spad(inc=None)

    e("vector", lambda eh: eh.memset(ZERO[:], 0.0))
    e("vector", lambda eh: eh.memset(EPS5[:], 1e-5))
    e("vector", lambda eh: eh.memset(EPS4[:], 1e-4))
    e("vector", lambda eh: eh.memset(PADT[:], 0.0))
    vb()
    w("vector", "LWC", wcnt["bs"])
    e("vector", lambda eh: eh.tensor_tensor(BADD[:], BADD[:], SCR[:, 0:1024], OP.add))
    vs_badd = vpad(inc=None)
    w("vector", "LWC", wcnt["winT"])
    for c in range(8):
        e("vector", lambda eh, cc=c: eh.tensor_scalar_mul(
            WIN[:, cc, :], WIN[:, cc, :], NWF[:, cc:cc + 1]))
    vs_winfold = vpad(inc=None)

    XB = [XA, SCR]
    xfree = [0] * (2 * NTILE)

    def emit_xload(slot, vs_guard):
        i = slot % NTILE
        b, t = i // 8, (i % 8) * TOK
        buf = XB[slot % 2]
        sem = ("LXA", "LXB")[slot % 2]
        if vs_guard:
            w("gpsimd", "QV", vs_guard)
        for blk in range(2):
            e("gpsimd", lambda eh, bb=b, tt=t + blk * 128, dst=buf[:, blk * 1024:(blk + 1) * 1024]:
              eh.dma_start(out=dst, in_=x_h[bb, tt:tt + 128, :]), inc=sem, k=16)

    def wait_x(eng, slot):
        w(eng, ("LXA", "LXB")[slot % 2], 32 * (slot // 2 + 1))

    def ln_stats(src):
        vb()
        for blk in range(2):
            for g in range(2):
                e("vector", lambda eh, b=blk, gg=g, s=src:
                  eh.bn_stats(ST6[:, 2 * b + gg, :],
                              s[:, b * 1024 + gg * 512: b * 1024 + (gg + 1) * 512]))
        vb()
        for blk in range(2):
            e("vector", lambda eh, b=blk: eh.bn_aggr(ST2[:, b, :], ST6[:, 2 * b:2 * b + 2, :]))

    def sqrt_recip(src_col_fn, eps):
        for blk in range(2):
            e("scalar", lambda eh, b=blk: eh.activation(
                SD[:, b:b + 1], src_col_fn(b), AF.Sqrt, bias=eps[:, 0:1]))
        ss = spad(inc=None)
        w("vector", "QS", ss)
        for blk in range(2):
            e("vector", lambda eh, b=blk: eh.reciprocal(RR[:, b:b + 1], SD[:, b:b + 1]))
        vb()

    def emit_tp_group(src, g):
        for blk in range(2):
            PT = (PT0, PT1)[blk]
            for j in range(4):
                c = 4 * g + j
                e("tensor", lambda eh, bb=blk, cc=c, jj=j, P=PT, s=src: eh.transpose(
                    P[:, jj, :], s[:, bb * 1024 + cc * 128: bb * 1024 + (cc + 1) * 128], EYE[:]))
            w("vector", "QT", cnt["QT"])
            e("vector", lambda eh, bb=blk, P=PT: eh.tensor_copy(
                XNT[:, :, bb * 128:(bb + 1) * 128], P[:, :, :]))
            ev = vpad(inc=None)
            w("tensor", "QV", ev)

    vs_uevac = [0] * NTILE
    emit_xload(0, 0)
    emit_xload(1, vs_badd)

    for i in range(NTILE):
        b, t = i // 8, (i % 8) * TOK
        buf = XB[i % 2]
        wait_x("vector", i)
        ln_stats(buf)
        vs_st = vpad(inc=None)
        w("scalar", "QV", vs_st)
        sqrt_recip(lambda blk: ST2[:, blk, 1:2], EPS5)
        if i == 0:
            w("vector", "LWC", wcnt["nw"])
        for blk in range(2):
            e("vector", lambda eh, bb=blk, s=buf: eh.tensor_scalar(
                T0[:, bb * 1024:(bb + 1) * 1024], s[:, bb * 1024:(bb + 1) * 1024],
                ST2[:, bb, 0:1], RR[:, bb:bb + 1], OP.subtract, OP.mult))
        ln_stats(T0)
        vb()
        for blk in range(2):
            e("vector", lambda eh, bb=blk: eh.tensor_tensor(
                MSQ[:, bb:bb + 1], ST2[:, bb, 0:1], ST2[:, bb, 0:1], OP.mult))
        vb()
        for blk in range(2):
            e("vector", lambda eh, bb=blk: eh.tensor_tensor(
                MS[:, bb:bb + 1], MSQ[:, bb:bb + 1], ST2[:, bb, 1:2], OP.add))
        vs_st2 = vpad(inc=None)
        xfree[i] = vs_st2
        if i + 2 < NTILE:
            emit_xload(i + 2, xfree[i])
        w("scalar", "QV", vs_st2)
        sqrt_recip(lambda blk: MS[:, blk:blk + 1], EPS4)
        for blk in range(2):
            e("vector", lambda eh, bb=blk: eh.tensor_scalar_mul(
                ATT[:, bb * 1024:(bb + 1) * 1024], T0[:, bb * 1024:(bb + 1) * 1024],
                RR[:, bb:bb + 1]))
        vs_xn = vpad(inc=None)

        if i == 0:
            w("tensor", "QV", vs_winfold)
        w("tensor", "QV", vs_xn)
        if i >= 1:
            w("tensor", "QV", vs_uevac[i - 1])
        for g in range(2):
            emit_tp_group(ATT, g)
            for nt in range(2):
                PO = (PU, PA)[nt]
                for c in range(4 * g, 4 * g + 4):
                    e("tensor", lambda eh, n=nt, cc=c, P=PO: eh.matmul(
                        P[:, 0:256], WIN[:, cc, n * 128:(n + 1) * 128],
                        XNT[:, cc % 4, :], start=(cc == 0), stop=(cc == 7)))
        w("vector", "QT", cnt["QT"])
        if i == 0:
            w("vector", "LWC", wcnt["bin"])
        for nt in range(2):
            PO = (PU, PA)[nt]
            e("vector", lambda eh, n=nt, bb=b, tt=t, P=PO: eh.tensor_scalar_add(
                U[:, n, bb * S + tt: bb * S + tt + TOK], P[:, 0:256],
                BIN[:, n:n + 1]))
        vs_uevac[i] = vpad(inc=None)

    w("vector", "LW", wcnt["wskipT"])
    for c in range(8):
        e("vector", lambda eh, cc=c: eh.tensor_scalar_mul(
            WSKIP[:, cc, :], WSKIP[:, cc, :], NWF[:, cc:cc + 1]))
    vpad()
    w("vector", "QS", ss_decs)
    vb()
    for nt in range(2):
        e("vector", lambda eh: eh.memset(SCR[:], 1.0))
        vb()
        e("vector", lambda eh, n=nt: eh.tensor_scalar_mul(SCR[:], SCR[:], DECS[:, n:n + 1]))
        vb()
        for bb in range(2):
            e("vector", lambda eh, n=nt, b2=bb: eh.tensor_tensor_scan(
                U[:, n, b2 * S:(b2 + 1) * S], SCR[:], U[:, n, b2 * S:(b2 + 1) * S],
                ZERO[:, 0:1], OP.mult, OP.add))
        vb()
    vs_scan = vpad(inc=None)
    w("tensor", "QV", vs_scan)
    emit_xload(NTILE + 0, xfree[14])

    os_done = [0] * NTILE
    for i in range(NTILE):
        b, t = i // 8, (i % 8) * TOK
        buf = XB[i % 2]
        if i + 1 < NTILE:
            emit_xload(NTILE + i + 1, vs_scan if i == 0 else xfree[NTILE + i - 1])
        wait_x("vector", NTILE + i)
        ln_stats(buf)
        vs_st = vpad(inc=None)
        w("scalar", "QV", vs_st)
        sqrt_recip(lambda blk: ST2[:, blk, 1:2], EPS5)
        if i >= 1:
            w("vector", "OS", os_done[i - 1])
        for blk in range(2):
            e("vector", lambda eh, bb=blk, s=buf: eh.tensor_scalar(
                T0[:, bb * 1024:(bb + 1) * 1024], s[:, bb * 1024:(bb + 1) * 1024],
                ST2[:, bb, 0:1], RR[:, bb:bb + 1], OP.subtract, OP.mult))
        ln_stats(T0)
        vb()
        for blk in range(2):
            e("vector", lambda eh, bb=blk: eh.tensor_tensor(
                MSQ[:, bb:bb + 1], ST2[:, bb, 0:1], ST2[:, bb, 0:1], OP.mult))
        vb()
        for blk in range(2):
            e("vector", lambda eh, bb=blk: eh.tensor_tensor(
                MS[:, bb:bb + 1], MSQ[:, bb:bb + 1], ST2[:, bb, 1:2], OP.add))
        vs_st2 = vpad(inc=None)
        w("scalar", "QV", vs_st2)
        sqrt_recip(lambda blk: MS[:, blk:blk + 1], EPS4)
        for blk in range(2):
            e("vector", lambda eh, bb=blk: eh.tensor_scalar_mul(
                ATT[:, bb * 1024:(bb + 1) * 1024], T0[:, bb * 1024:(bb + 1) * 1024],
                RR[:, bb:bb + 1]))
        vs_xn = vpad(inc=None)

        w("tensor", "QV", vs_xn)
        COMBO = [(0, 0, PA), (0, 1, PB), (1, 0, PU), (1, 1, PP)]
        for g in range(2):
            emit_tp_group(ATT, g)
            if g == 0:
                for blk, half, PO in COMBO:
                    for nt in range(2):
                        e("tensor", lambda eh, n=nt, bb=b, ttk=t + blk * 128, hh=half, P=PO: eh.matmul(
                            P[:], U[:, n, bb * S + ttk: bb * S + ttk + 128],
                            WOUT[:, n, hh * 512:(hh + 1) * 512],
                            start=(n == 0), stop=False))
            for blk, half, PO in COMBO:
                for c in range(4 * g, 4 * g + 4):
                    e("tensor", lambda eh, cc=c, bb=blk, hh=half, P=PO: eh.matmul(
                        P[:], XNT[:, cc % 4, bb * 128:(bb + 1) * 128],
                        WSKIP[:, cc, hh * 512:(hh + 1) * 512],
                        start=False, stop=(cc == 7)))
        w("vector", "QT", cnt["QT"])
        if i == 0:
            w("vector", "LWC", wcnt["b2"])
        for blk, half, PO in COMBO:
            e("vector", lambda eh, bb=blk, hh=half, P=PO: eh.tensor_tensor(
                ATT[:, bb * 1024 + hh * 512: bb * 1024 + (hh + 1) * 512], P[:],
                T0[:, bb * 1024 + hh * 512: bb * 1024 + (hh + 1) * 512], OP.add))
        vb()
        for blk, half, PO in COMBO:
            e("vector", lambda eh, bb=blk, hh=half: eh.tensor_tensor(
                ATT[:, bb * 1024 + hh * 512: bb * 1024 + (hh + 1) * 512],
                ATT[:, bb * 1024 + hh * 512: bb * 1024 + (hh + 1) * 512],
                BADD[:, hh * 512:(hh + 1) * 512], OP.add))
        ev = vpad(inc=None)
        w("tensor", "QV", ev)

        ln_stats(ATT)
        vs_st = vpad(inc=None)
        w("scalar", "QV", vs_st)
        sqrt_recip(lambda blk: ST2[:, blk, 1:2], EPS5)
        for blk in range(2):
            e("vector", lambda eh, bb=blk: eh.tensor_scalar(
                T0[:, bb * 1024:(bb + 1) * 1024], ATT[:, bb * 1024:(bb + 1) * 1024],
                ST2[:, bb, 0:1], RR[:, bb:bb + 1], OP.subtract, OP.mult))
        vb()
        for blk in range(2):
            e("vector", lambda eh, bb=blk, s=buf: eh.tensor_tensor(
                s[:, bb * 1024:(bb + 1) * 1024], s[:, bb * 1024:(bb + 1) * 1024],
                T0[:, bb * 1024:(bb + 1) * 1024], OP.add))
        ln_stats(buf)
        vs_st = vpad(inc=None)
        w("scalar", "QV", vs_st)
        sqrt_recip(lambda blk: ST2[:, blk, 1:2], EPS5)
        for blk in range(2):
            e("vector", lambda eh, bb=blk, s=buf: eh.tensor_scalar(
                T0[:, bb * 1024:(bb + 1) * 1024], s[:, bb * 1024:(bb + 1) * 1024],
                ST2[:, bb, 0:1], RR[:, bb:bb + 1], OP.subtract, OP.mult))
        vs_t1 = vpad(inc=None)

        w("tensor", "QV", vs_t1)
        if i == 0:
            w("tensor", "LW2", 64)
        for g in range(2):
            emit_tp_group(T0, g)
            for r in range(2):
                PO = (PU, PA)[r]
                for c in range(4 * g, 4 * g + 4):
                    e("tensor", lambda eh, rr=r, cc=c, P=PO: eh.matmul(
                        P[:, 0:256], U1[:, cc, rr * 128:(rr + 1) * 128],
                        XNT[:, cc % 4, :], start=(cc == 0), stop=(cc == 7)))
        w("vector", "QT", cnt["QT"])
        for r in range(2):
            PO = (PU, PA)[r]
            e("vector", lambda eh, rr=r, P=PO: eh.tensor_copy(
                ATT[:, rr * 256:(rr + 1) * 256], P[:, 0:256]))
        vs_p = vpad(inc=None)
        w("tensor", "QV", vs_p)

        gelu_mark = [0] * 32
        for c in range(33):
            if c < 32:
                if c >= 2:
                    w("tensor", "QS", gelu_mark[c - 2])
                for r in range(2):
                    e("tensor", lambda eh, rr=r, cc=c: eh.matmul(
                        PH[:, cc % 2, :], V1[:, rr, cc * 128:(cc + 1) * 128],
                        ATT[:, rr * 256:(rr + 1) * 256],
                        start=(rr == 0), stop=(rr == 1)))
                w("scalar", "QT", cnt["QT"])
                if i == 0 and c == 0:
                    w("scalar", "LWC", wcnt["b1"])
                e("scalar", lambda eh, cc=c: eh.activation(
                    ATT[:, 1536 + (cc % 2) * 256: 1536 + (cc % 2 + 1) * 256],
                    PH[:, cc % 2, :], AF.Gelu_apprx_tanh, bias=B1[:, cc:cc + 1]))
                gelu_mark[c] = spad()
            if c >= 1:
                cc0 = c - 1
                w("tensor", "QS", gelu_mark[cc0])
                for r in range(2):
                    PO = PQ[:, 0, :] if r == 0 else PB[:, 0:256]
                    e("tensor", lambda eh, rr=r, cc=cc0, P=PO: eh.matmul(
                        P, U2[:, cc, rr * 128:(rr + 1) * 128],
                        ATT[:, 1536 + (cc % 2) * 256: 1536 + (cc % 2 + 1) * 256],
                        start=(cc == 0), stop=(cc == 31)))
        w("vector", "QT", cnt["QT"])
        for r in range(2):
            PO = PQ[:, 0, :] if r == 0 else PB[:, 0:256]
            e("vector", lambda eh, rr=r, P=PO: eh.tensor_copy(
                ATT[:, 1024 + rr * 256:1024 + (rr + 1) * 256], P))
        vs_q = vpad(inc=None)
        w("tensor", "QV", vs_q)

        for blk in range(2):
            for half in range(2):
                PO = (PA, PB)[half]
                for r in range(2):
                    e("tensor", lambda eh, rr=r, bb=blk, hh=half, P=PO: eh.matmul(
                        P[:], ATT[:, 1024 + rr * 256 + bb * 128: 1024 + rr * 256 + (bb + 1) * 128],
                        V2[:, rr, hh * 512:(hh + 1) * 512],
                        start=(rr == 0), stop=(rr == 1)))
            w("vector", "QT", cnt["QT"])
            for half in range(2):
                PO = (PA, PB)[half]
                e("vector", lambda eh, bb=blk, hh=half, P=PO: eh.tensor_tensor(
                    ATT[:, bb * 1024 + hh * 512: bb * 1024 + (hh + 1) * 512], P[:],
                    B2R[:, hh * 512:(hh + 1) * 512], OP.add))
            ev = vpad(inc=None)
            w("tensor", "QV", ev)
        ln_stats(ATT)
        vs_st = vpad(inc=None)
        w("scalar", "QV", vs_st)
        sqrt_recip(lambda blk: ST2[:, blk, 1:2], EPS5)
        for blk in range(2):
            e("vector", lambda eh, bb=blk: eh.tensor_scalar(
                T0[:, bb * 1024:(bb + 1) * 1024], ATT[:, bb * 1024:(bb + 1) * 1024],
                ST2[:, bb, 0:1], RR[:, bb:bb + 1], OP.subtract, OP.mult))
        vb()
        for blk in range(2):
            e("vector", lambda eh, bb=blk, s=buf: eh.tensor_tensor(
                T0[:, bb * 1024:(bb + 1) * 1024], T0[:, bb * 1024:(bb + 1) * 1024],
                s[:, bb * 1024:(bb + 1) * 1024], OP.add))
        evf = vpad(inc=None)
        xfree[NTILE + i] = evf
        w("gpsimd", "QV", evf)
        for blk in range(2):
            e("gpsimd", lambda eh, bb=b, ttk=t + blk * 128, o=blk: eh.dma_start(
                out=o_h[bb, ttk:ttk + 128, :], in_=T0[:, o * 1024:(o + 1) * 1024]),
              inc="OS", k=16)
        os_done[i] = cnt["OS"]

    w("gpsimd", "OS", cnt["OS"])

    with es, nc.Block() as block:
        @block.sync
        def _(sync):
            for f in ops["sync"]:
                f(sync)

        @block.gpsimd
        def _(gpsimd):
            for f in ops["gpsimd"]:
                f(gpsimd)

        @block.vector
        def _(vector):
            for f in ops["vector"]:
                f(vector)

        @block.scalar
        def _(scalar):
            for f in ops["scalar"]:
                f(scalar)

        @block.tensor
        def _(tensor):
            for f in ops["tensor"]:
                f(tensor)
    return nc


def kernel(L0_attn_tokens, L0_attn_norm_w, L0_attn_W_in, L0_attn_b_in,
           L0_attn_W_out, L0_attn_b_out, L0_attn_W_skip, L0_attn_b_skip,
           L0_attn_log_decay, L0_ffn_U1, L0_ffn_V1, L0_ffn_b1, L0_ffn_U2,
           L0_ffn_V2, L0_ffn_b2):
    f = np.float32
    tok = np.ascontiguousarray(np.asarray(L0_attn_tokens, f))
    rep = lambda v: np.ascontiguousarray(np.broadcast_to(np.asarray(v, f), (128, 1024)))
    chunked = lambda wmat, nch, cols: np.ascontiguousarray(
        np.asarray(wmat, f).reshape(nch, 128, cols).transpose(1, 0, 2))
    com = {
        "winT": chunked(np.asarray(L0_attn_W_in, f).T, 8, 256),
        "woutT": chunked(np.asarray(L0_attn_W_out, f).T, 2, 1024),
        "wskipT": chunked(np.asarray(L0_attn_W_skip, f).T, 8, 1024),
        "u1": chunked(L0_ffn_U1, 8, 256),
        "v1": chunked(L0_ffn_V1, 2, 4096),
        "u2": chunked(L0_ffn_U2, 32, 256),
        "v2": chunked(L0_ffn_V2, 2, 1024),
        "nwf": np.ascontiguousarray(np.asarray(L0_attn_norm_w, f).reshape(8, 128).T),
        "bo": rep(L0_attn_b_out),
        "bs": rep(L0_attn_b_skip),
        "b2": rep(L0_ffn_b2),
        "bin": np.ascontiguousarray(np.asarray(L0_attn_b_in, f).reshape(2, 128).T),
        "b1": np.ascontiguousarray(np.asarray(L0_ffn_b1, f).reshape(32, 128).T),
        "ldec": np.ascontiguousarray(np.asarray(L0_attn_log_decay, f).reshape(2, 128).T),
        "eye": np.eye(128, dtype=f),
    }
    in_maps = []
    for c in range(8):
        m = dict(com)
        m["x"] = np.ascontiguousarray(tok[2 * c:2 * c + 2])
        in_maps.append(m)
    nc = build_nc()
    res = run_bass_kernel_spmd(nc, in_maps, core_ids=list(range(8)))
    out = np.concatenate([res.results[c]["o"] for c in range(8)], axis=0)
    return np.ascontiguousarray(out.astype(np.float32))


# version 15
# speedup vs baseline: 1.1427x; 1.0255x over previous
import numpy as np
from contextlib import ExitStack

import concourse.bass as bass
import concourse.mybir as mybir
from concourse.bass_utils import run_bass_kernel_spmd

F32 = mybir.dt.float32
AF = mybir.ActivationFunctionType
OP = mybir.AluOpType

B_LOC = 2
S = 2048
D = 1024
TOK = 256
NTILE = B_LOC * S // TOK
ENGS = ("sync", "gpsimd", "vector", "scalar", "tensor")


def build_nc():
    nc = bass.Bass("TRN2", target_bir_lowering=False)
    x_h = nc.dram_tensor("x", [B_LOC, S, D], F32, kind="ExternalInput").ap()
    o_h = nc.dram_tensor("o", [B_LOC, S, D], F32, kind="ExternalOutput").ap()
    winT_h = nc.dram_tensor("winT", [128, 8, 256], F32, kind="ExternalInput").ap()
    woutT_h = nc.dram_tensor("woutT", [128, 2, 1024], F32, kind="ExternalInput").ap()
    wskipT_h = nc.dram_tensor("wskipT", [128, 8, 1024], F32, kind="ExternalInput").ap()
    u1_h = nc.dram_tensor("u1", [128, 8, 256], F32, kind="ExternalInput").ap()
    v1_h = nc.dram_tensor("v1", [128, 2, 4096], F32, kind="ExternalInput").ap()
    u2_h = nc.dram_tensor("u2", [128, 32, 256], F32, kind="ExternalInput").ap()
    v2_h = nc.dram_tensor("v2", [128, 2, 1024], F32, kind="ExternalInput").ap()
    nwf_h = nc.dram_tensor("nwf", [128, 8], F32, kind="ExternalInput").ap()
    bo_h = nc.dram_tensor("bo", [128, 1024], F32, kind="ExternalInput").ap()
    bs_h = nc.dram_tensor("bs", [128, 1024], F32, kind="ExternalInput").ap()
    b2_h = nc.dram_tensor("b2", [128, 1024], F32, kind="ExternalInput").ap()
    bin_h = nc.dram_tensor("bin", [128, 2], F32, kind="ExternalInput").ap()
    b1_h = nc.dram_tensor("b1", [128, 32], F32, kind="ExternalInput").ap()
    ldec_h = nc.dram_tensor("ldec", [128, 2], F32, kind="ExternalInput").ap()
    eye_h = nc.dram_tensor("eye", [128, 128], F32, kind="ExternalInput").ap()

    es = ExitStack()
    sb = lambda name, shape: es.enter_context(nc.sbuf_tensor(name, shape, F32))
    WIN = sb("WIN", [128, 8, 256])
    WOUT = sb("WOUT", [128, 2, 1024])
    WSKIP = sb("WSKIP", [128, 8, 1024])
    V1 = sb("V1", [128, 2, 4096])
    U2 = sb("U2", [128, 32, 256])
    V2 = sb("V2", [128, 2, 1024])
    BADD = sb("BADD", [128, 1024])
    B2R = sb("B2R", [128, 1024])
    BIN = sb("BIN", [128, 2])
    B1 = sb("B1", [128, 32])
    LDEC = sb("LDEC", [128, 2])
    NWF = sb("NWF", [128, 8])
    DECS = sb("DECS", [128, 2])
    EYE = sb("EYE", [128, 128])
    U = sb("U", [128, 2, 2 * S])
    XA = sb("XA", [128, 2048])
    SCR = sb("SCR", [128, 2048])
    T0 = sb("T0", [128, 2048])
    ATT = sb("ATT", [128, 2048])
    FFNB = sb("FFNB", [128, 2048])
    XNT = sb("XNT", [128, 4, 256])
    ST6 = sb("ST6", [128, 4, 6])
    ST2 = sb("ST2", [128, 2, 2])
    MSQ = sb("MSQ", [128, 2])
    MS = sb("MS", [128, 2])
    SD = sb("SD", [128, 2])
    RR = sb("RR", [128, 2])
    ZERO = sb("ZERO", [128, 1])
    EPS5 = sb("EPS5", [128, 1])
    EPS4 = sb("EPS4", [128, 1])
    PADT = sb("PADT", [128, 64])
    PADS = sb("PADS", [128, 64])

    ps = lambda name, shape: es.enter_context(nc.psum_tensor(name, shape, F32))
    PT0 = ps("PT0", [128, 4, 128])
    PT1 = ps("PT1", [128, 4, 128])
    PA = ps("PA", [128, 512])
    PB = ps("PB", [128, 512])
    PU = ps("PU", [128, 512])
    PP = ps("PP", [128, 512])
    PH = ps("PH", [128, 2, 256])
    PQ = ps("PQ", [128, 2, 256])

    sem_names = ("LXA", "LXB", "LWC", "LW", "LW2", "QV", "QS", "QT", "OS")
    semh = {s: es.enter_context(nc.semaphore(s)) for s in sem_names}

    ops = {e: [] for e in ENGS}
    cnt = {s: 0 for s in sem_names}

    def w(eng, sem, val):
        if val > 0:
            ops[eng].append(lambda eh, sm=sem, v=val: eh.wait_ge(semh[sm], v))

    AUTO = {"vector": "QV", "scalar": "QS", "tensor": "QT"}

    def e(eng, fn, inc=None, k=1):
        if inc is None and eng in AUTO:
            inc, k = AUTO[eng], 1
        if inc is None:
            ops[eng].append(fn)
            return None
        cnt[inc] += k
        ops[eng].append(lambda eh, f=fn, s=inc, kk=k: f(eh).then_inc(semh[s], kk))
        return cnt[inc]

    def vb():
        ops["vector"].append(lambda eh, v=cnt["QV"]: eh.wait_ge(semh["QV"], v))

    _pv = [0]

    def vpad(inc=None):
        j1 = _pv[0] % 64
        j2 = (_pv[0] + 1) % 64
        _pv[0] += 2
        e("vector", lambda eh, jj=j1: eh.tensor_copy(PADT[:, jj:jj + 1], ZERO[:]))
        return e("vector", lambda eh, jj=j2: eh.tensor_copy(PADT[:, jj:jj + 1], ZERO[:]))

    _ps = [0]

    def spad(inc=None):
        j = _ps[0] % 64
        _ps[0] += 1
        return e("scalar", lambda eh, jj=j: eh.activation(PADS[:, jj:jj + 1], ZERO[:], AF.Copy))

    for name, dst, src in (
        ("eye", EYE[:], eye_h[:]), ("nw", NWF[:], nwf_h[:]), ("bin", BIN[:], bin_h[:]),
        ("b1", B1[:], b1_h[:]), ("ldec", LDEC[:], ldec_h[:]), ("bo", BADD[:], bo_h[:]),
        ("bs", SCR[:, 0:1024], bs_h[:]), ("b2", B2R[:], b2_h[:]),
        ("winT", WIN[:], winT_h[:]),
    ):
        e("sync", lambda eh, d=dst, s=src: eh.dma_start(out=d, in_=s), inc="LWC", k=16)
    for name, dst, src in (("woutT", WOUT[:], woutT_h[:]), ("wskipT", WSKIP[:], wskipT_h[:])):
        e("sync", lambda eh, d=dst, s=src: eh.dma_start(out=d, in_=s), inc="LW", k=16)
    LWC_ALL, LW_ALL = 144, 32
    wcnt = {k: LWC_ALL for k in ("eye", "nw", "bin", "b1", "ldec", "bo", "bs", "b2", "winT")}
    wcnt.update({"woutT": LW_ALL, "wskipT": LW_ALL})
    w2cnt = {}
    for name, dst, src in (("v1", V1[:], v1_h[:]), ("u2", U2[:], u2_h[:]),
                           ("v2", V2[:], v2_h[:])):
        w2cnt[name] = e("scalar", lambda eh, d=dst, s=src: eh.dma_start(out=d, in_=s), inc="LW2", k=16)

    w("scalar", "LWC", wcnt["ldec"])
    ops["scalar"].append(lambda eh: eh.wait_ge(semh["QV"], 4))
    e("scalar", lambda eh: eh.activation(DECS[:], LDEC[:], AF.Sigmoid))
    ss_decs = spad(inc=None)

    e("vector", lambda eh: eh.memset(ZERO[:], 0.0))
    e("vector", lambda eh: eh.memset(EPS5[:], 1e-5))
    e("vector", lambda eh: eh.memset(EPS4[:], 1e-4))
    e("vector", lambda eh: eh.memset(PADT[:], 0.0))
    vb()
    w("vector", "LWC", wcnt["bs"])
    e("vector", lambda eh: eh.tensor_tensor(BADD[:], BADD[:], SCR[:, 0:1024], OP.add))
    vs_badd = vpad(inc=None)
    w("vector", "LWC", wcnt["winT"])
    for c in range(8):
        e("vector", lambda eh, cc=c: eh.tensor_scalar_mul(
            WIN[:, cc, :], WIN[:, cc, :], NWF[:, cc:cc + 1]))
    vs_winfold = vpad(inc=None)

    XB = [XA, SCR]
    xfree = [0] * (2 * NTILE + 2)

    def emit_xload(slot, vs_guard):
        i = slot % NTILE
        b, t = i // 8, (i % 8) * TOK
        buf = XB[slot % 2]
        sem = ("LXA", "LXB")[slot % 2]
        if vs_guard:
            w("gpsimd", "QV", vs_guard)
        for blk in range(2):
            e("gpsimd", lambda eh, bb=b, tt=t + blk * 128, dst=buf[:, blk * 1024:(blk + 1) * 1024]:
              eh.dma_start(out=dst, in_=x_h[bb, tt:tt + 128, :]), inc=sem, k=16)

    def wait_x(eng, slot):
        w(eng, ("LXA", "LXB")[slot % 2], 32 * (slot // 2 + 1))

    def ln_stats(src):
        vb()
        for blk in range(2):
            for g in range(2):
                e("vector", lambda eh, b=blk, gg=g, s=src:
                  eh.bn_stats(ST6[:, 2 * b + gg, :],
                              s[:, b * 1024 + gg * 512: b * 1024 + (gg + 1) * 512]))
        vb()
        for blk in range(2):
            e("vector", lambda eh, b=blk: eh.bn_aggr(ST2[:, b, :], ST6[:, 2 * b:2 * b + 2, :]))

    def sqrt_recip(src_col_fn, eps):
        for blk in range(2):
            e("scalar", lambda eh, b=blk: eh.activation(
                SD[:, b:b + 1], src_col_fn(b), AF.Sqrt, bias=eps[:, 0:1]))
        ss = spad(inc=None)
        w("vector", "QS", ss)
        for blk in range(2):
            e("vector", lambda eh, b=blk: eh.reciprocal(RR[:, b:b + 1], SD[:, b:b + 1]))
        vb()

    def emit_tp_group(src, g):
        for blk in range(2):
            PT = (PT0, PT1)[blk]
            for j in range(4):
                c = 4 * g + j
                e("tensor", lambda eh, bb=blk, cc=c, jj=j, P=PT, s=src: eh.transpose(
                    P[:, jj, :], s[:, bb * 1024 + cc * 128: bb * 1024 + (cc + 1) * 128], EYE[:]))
            w("vector", "QT", cnt["QT"])
            e("vector", lambda eh, bb=blk, P=PT: eh.tensor_copy(
                XNT[:, :, bb * 128:(bb + 1) * 128], P[:, :, :]))
            ev = vpad(inc=None)
            w("tensor", "QV", ev)

    def vfront(slot, xn_dst, tp_guard_ts=0):
        buf = XB[slot % 2]
        wait_x("vector", slot)
        vs_st = None
        ln_stats(buf)
        vs_st = vpad(inc=None)
        w("scalar", "QV", vs_st)
        sqrt_recip(lambda blk: ST2[:, blk, 1:2], EPS5)
        for blk in range(2):
            e("vector", lambda eh, bb=blk, s=buf: eh.tensor_scalar(
                T0[:, bb * 1024:(bb + 1) * 1024], s[:, bb * 1024:(bb + 1) * 1024],
                ST2[:, bb, 0:1], RR[:, bb:bb + 1], OP.subtract, OP.mult))
        ln_stats(T0)
        vb()
        for blk in range(2):
            e("vector", lambda eh, bb=blk: eh.tensor_tensor(
                MSQ[:, bb:bb + 1], ST2[:, bb, 0:1], ST2[:, bb, 0:1], OP.mult))
        vb()
        for blk in range(2):
            e("vector", lambda eh, bb=blk: eh.tensor_tensor(
                MS[:, bb:bb + 1], MSQ[:, bb:bb + 1], ST2[:, bb, 1:2], OP.add))
        vs_st2 = vpad(inc=None)
        w("scalar", "QV", vs_st2)
        sqrt_recip(lambda blk: MS[:, blk:blk + 1], EPS4)
        if tp_guard_ts:
            w("vector", "QT", tp_guard_ts)
        for blk in range(2):
            e("vector", lambda eh, bb=blk, d=xn_dst: eh.tensor_scalar_mul(
                d[:, bb * 1024:(bb + 1) * 1024], T0[:, bb * 1024:(bb + 1) * 1024],
                RR[:, bb:bb + 1]))
        return vpad(inc=None)

    vs_uevac = [0] * NTILE
    vs_xn1 = [0] * NTILE
    ts_tp1 = [0] * NTILE

    emit_xload(0, 0)
    emit_xload(1, vs_badd)
    xnb = lambda i: (ATT, FFNB)[i % 2]
    vs_xn1[0] = vfront(0, xnb(0))
    xfree[0] = vs_xn1[0]
    emit_xload(2, xfree[0])
    vs_xn1[1] = vfront(1, xnb(1))
    xfree[1] = vs_xn1[1]
    emit_xload(3, xfree[1])

    for i in range(NTILE):
        b, t = i // 8, (i % 8) * TOK
        if i == 0:
            w("tensor", "QV", vs_winfold)
        w("tensor", "QV", vs_xn1[i])
        if i >= 1:
            w("tensor", "QV", vs_uevac[i - 1])
        for g in range(2):
            emit_tp_group(xnb(i), g)
            for nt in range(2):
                PO = (PU, PA)[nt]
                for c in range(4 * g, 4 * g + 4):
                    e("tensor", lambda eh, n=nt, cc=c, P=PO: eh.matmul(
                        P[:, 0:256], WIN[:, cc, n * 128:(n + 1) * 128],
                        XNT[:, cc % 4, :], start=(cc == 0), stop=(cc == 7)))
        ts_tp1[i] = cnt["QT"]
        w("vector", "QT", ts_tp1[i])
        if i == 0:
            w("vector", "LWC", wcnt["bin"])
        for nt in range(2):
            PO = (PU, PA)[nt]
            e("vector", lambda eh, n=nt, bb=b, tt=t, P=PO: eh.tensor_scalar_add(
                U[:, n, bb * S + tt: bb * S + tt + TOK], P[:, 0:256],
                BIN[:, n:n + 1]))
        vs_uevac[i] = vpad(inc=None)
        if i + 2 < NTILE:
            vs_xn1[i + 2] = vfront(i + 2, xnb(i + 2), tp_guard_ts=ts_tp1[i])
            xfree[i + 2] = vs_xn1[i + 2]
            if i + 4 < NTILE:
                emit_xload(i + 4, xfree[i + 2])

    w("vector", "LW", wcnt["wskipT"])
    for c in range(8):
        e("vector", lambda eh, cc=c: eh.tensor_scalar_mul(
            WSKIP[:, cc, :], WSKIP[:, cc, :], NWF[:, cc:cc + 1]))
    vpad()
    w("scalar", "QT", ts_tp1[NTILE - 1])
    w2cnt["u1"] = e("scalar", lambda eh: eh.dma_start(out=WIN[:], in_=u1_h[:]), inc="LW2", k=16)

    w("vector", "QS", ss_decs)
    vb()
    for nt in range(2):
        e("vector", lambda eh: eh.memset(SCR[:], 1.0))
        vb()
        e("vector", lambda eh, n=nt: eh.tensor_scalar_mul(SCR[:], SCR[:], DECS[:, n:n + 1]))
        vb()
        for bb in range(2):
            e("vector", lambda eh, n=nt, b2=bb: eh.tensor_tensor_scan(
                U[:, n, b2 * S:(b2 + 1) * S], SCR[:], U[:, n, b2 * S:(b2 + 1) * S],
                ZERO[:, 0:1], OP.mult, OP.add))
        vb()
    vs_scan = vpad(inc=None)
    w("tensor", "QV", vs_scan)
    emit_xload(NTILE + 0, xfree[14])

    os_done = [0] * NTILE

    def vfront2(i):
        slot = NTILE + i
        buf = XB[i % 2]
        wait_x("vector", slot)
        ln_stats(buf)
        vs_st = vpad(inc=None)
        w("scalar", "QV", vs_st)
        sqrt_recip(lambda blk: ST2[:, blk, 1:2], EPS5)
        for blk in range(2):
            e("vector", lambda eh, bb=blk, s=buf: eh.tensor_scalar(
                T0[:, bb * 1024:(bb + 1) * 1024], s[:, bb * 1024:(bb + 1) * 1024],
                ST2[:, bb, 0:1], RR[:, bb:bb + 1], OP.subtract, OP.mult))
        ln_stats(T0)
        vb()
        for blk in range(2):
            e("vector", lambda eh, bb=blk: eh.tensor_tensor(
                MSQ[:, bb:bb + 1], ST2[:, bb, 0:1], ST2[:, bb, 0:1], OP.mult))
        vb()
        for blk in range(2):
            e("vector", lambda eh, bb=blk: eh.tensor_tensor(
                MS[:, bb:bb + 1], MSQ[:, bb:bb + 1], ST2[:, bb, 1:2], OP.add))
        vs_st2 = vpad(inc=None)
        w("scalar", "QV", vs_st2)
        sqrt_recip(lambda blk: MS[:, blk:blk + 1], EPS4)
        for blk in range(2):
            e("vector", lambda eh, bb=blk: eh.tensor_scalar_mul(
                ATT[:, bb * 1024:(bb + 1) * 1024], T0[:, bb * 1024:(bb + 1) * 1024],
                RR[:, bb:bb + 1]))
        return vpad(inc=None)

    vs_xn2 = [0] * NTILE
    vs_xn2[0] = vfront2(0)
    emit_xload(NTILE + 1, vs_scan)

    COMBO = [(0, 0, PA), (0, 1, PB), (1, 0, PU), (1, 1, PP)]
    for i in range(NTILE):
        b, t = i // 8, (i % 8) * TOK
        buf = XB[i % 2]
        if 1 <= i and i + 1 < NTILE:
            emit_xload(NTILE + i + 1, xfree[NTILE + i - 1])

        w("tensor", "QV", vs_xn2[i])
        for g in range(2):
            emit_tp_group(ATT, g)
            if g == 0:
                for blk, half, PO in COMBO:
                    for nt in range(2):
                        e("tensor", lambda eh, n=nt, bb=b, ttk=t + blk * 128, hh=half, P=PO: eh.matmul(
                            P[:], U[:, n, bb * S + ttk: bb * S + ttk + 128],
                            WOUT[:, n, hh * 512:(hh + 1) * 512],
                            start=(n == 0), stop=False))
            for blk, half, PO in COMBO:
                for c in range(4 * g, 4 * g + 4):
                    e("tensor", lambda eh, cc=c, bb=blk, hh=half, P=PO: eh.matmul(
                        P[:], XNT[:, cc % 4, bb * 128:(bb + 1) * 128],
                        WSKIP[:, cc, hh * 512:(hh + 1) * 512],
                        start=False, stop=(cc == 7)))
        ts_y = cnt["QT"]

        w("vector", "QT", ts_y)
        if i == 0:
            w("vector", "LWC", wcnt["b2"])
        for blk, half, PO in COMBO:
            e("vector", lambda eh, bb=blk, hh=half, P=PO: eh.tensor_tensor(
                ATT[:, bb * 1024 + hh * 512: bb * 1024 + (hh + 1) * 512], P[:],
                T0[:, bb * 1024 + hh * 512: bb * 1024 + (hh + 1) * 512], OP.add))
        vb()
        for blk, half, PO in COMBO:
            e("vector", lambda eh, bb=blk, hh=half: eh.tensor_tensor(
                ATT[:, bb * 1024 + hh * 512: bb * 1024 + (hh + 1) * 512],
                ATT[:, bb * 1024 + hh * 512: bb * 1024 + (hh + 1) * 512],
                BADD[:, hh * 512:(hh + 1) * 512], OP.add))
        ev = vpad(inc=None)
        w("tensor", "QV", ev)
        ln_stats(ATT)
        vs_st = vpad(inc=None)
        w("scalar", "QV", vs_st)
        sqrt_recip(lambda blk: ST2[:, blk, 1:2], EPS5)
        for blk in range(2):
            e("vector", lambda eh, bb=blk: eh.tensor_scalar(
                T0[:, bb * 1024:(bb + 1) * 1024], ATT[:, bb * 1024:(bb + 1) * 1024],
                ST2[:, bb, 0:1], RR[:, bb:bb + 1], OP.subtract, OP.mult))
        vb()
        for blk in range(2):
            e("vector", lambda eh, bb=blk, s=buf: eh.tensor_tensor(
                s[:, bb * 1024:(bb + 1) * 1024], s[:, bb * 1024:(bb + 1) * 1024],
                T0[:, bb * 1024:(bb + 1) * 1024], OP.add))
        ln_stats(buf)
        vs_st = vpad(inc=None)
        w("scalar", "QV", vs_st)
        sqrt_recip(lambda blk: ST2[:, blk, 1:2], EPS5)
        for blk in range(2):
            e("vector", lambda eh, bb=blk, s=buf: eh.tensor_scalar(
                T0[:, bb * 1024:(bb + 1) * 1024], s[:, bb * 1024:(bb + 1) * 1024],
                ST2[:, bb, 0:1], RR[:, bb:bb + 1], OP.subtract, OP.mult))
        vs_t1 = vpad(inc=None)

        w("tensor", "QV", vs_t1)
        if i == 0:
            w("tensor", "LW2", 64)
        for g in range(2):
            emit_tp_group(T0, g)
            for r in range(2):
                PO = (PU, PA)[r]
                for c in range(4 * g, 4 * g + 4):
                    e("tensor", lambda eh, rr=r, cc=c, P=PO: eh.matmul(
                        P[:, 0:256], WIN[:, cc, rr * 128:(rr + 1) * 128],
                        XNT[:, cc % 4, :], start=(cc == 0), stop=(cc == 7)))
        w("vector", "QT", cnt["QT"])
        if i >= 1:
            w("vector", "OS", os_done[i - 1])
        for r in range(2):
            PO = (PU, PA)[r]
            e("vector", lambda eh, rr=r, P=PO: eh.tensor_copy(
                FFNB[:, rr * 256:(rr + 1) * 256], P[:, 0:256]))
        vs_p = vpad(inc=None)
        w("tensor", "QV", vs_p)

        if i + 1 < NTILE:
            vs_xn2[i + 1] = vfront2(i + 1)

        gelu_mark = [0] * 32
        for c in range(33):
            if c < 32:
                if c >= 2:
                    w("tensor", "QS", gelu_mark[c - 2])
                for r in range(2):
                    e("tensor", lambda eh, rr=r, cc=c: eh.matmul(
                        PH[:, cc % 2, :], V1[:, rr, cc * 128:(cc + 1) * 128],
                        FFNB[:, rr * 256:(rr + 1) * 256],
                        start=(rr == 0), stop=(rr == 1)))
                w("scalar", "QT", cnt["QT"])
                if i == 0 and c == 0:
                    w("scalar", "LWC", wcnt["b1"])
                e("scalar", lambda eh, cc=c: eh.activation(
                    FFNB[:, 1536 + (cc % 2) * 256: 1536 + (cc % 2 + 1) * 256],
                    PH[:, cc % 2, :], AF.Gelu_apprx_tanh, bias=B1[:, cc:cc + 1]))
                gelu_mark[c] = spad()
            if c >= 1:
                cc0 = c - 1
                w("tensor", "QS", gelu_mark[cc0])
                for r in range(2):
                    PO = PQ[:, 0, :] if r == 0 else PB[:, 0:256]
                    e("tensor", lambda eh, rr=r, cc=cc0, P=PO: eh.matmul(
                        P, U2[:, cc, rr * 128:(rr + 1) * 128],
                        FFNB[:, 1536 + (cc % 2) * 256: 1536 + (cc % 2 + 1) * 256],
                        start=(cc == 0), stop=(cc == 31)))
        ts_q = cnt["QT"]

        w("vector", "QT", ts_q)
        for r in range(2):
            PO = PQ[:, 0, :] if r == 0 else PB[:, 0:256]
            e("vector", lambda eh, rr=r, P=PO: eh.tensor_copy(
                FFNB[:, 1024 + rr * 256:1024 + (rr + 1) * 256], P))
        vs_q = vpad(inc=None)
        w("tensor", "QV", vs_q)
        for blk in range(2):
            for half in range(2):
                PO = (PA, PB)[half]
                for r in range(2):
                    e("tensor", lambda eh, rr=r, bb=blk, hh=half, P=PO: eh.matmul(
                        P[:], FFNB[:, 1024 + rr * 256 + bb * 128: 1024 + rr * 256 + (bb + 1) * 128],
                        V2[:, rr, hh * 512:(hh + 1) * 512],
                        start=(rr == 0), stop=(rr == 1)))
            w("vector", "QT", cnt["QT"])
            for half in range(2):
                PO = (PA, PB)[half]
                e("vector", lambda eh, bb=blk, hh=half, P=PO: eh.tensor_tensor(
                    FFNB[:, bb * 1024 + hh * 512: bb * 1024 + (hh + 1) * 512], P[:],
                    B2R[:, hh * 512:(hh + 1) * 512], OP.add))
            evb = vpad(inc=None)
            w("tensor", "QV", evb)
        ln_stats(FFNB)
        vs_st = vpad(inc=None)
        w("scalar", "QV", vs_st)
        sqrt_recip(lambda blk: ST2[:, blk, 1:2], EPS5)
        for blk in range(2):
            e("vector", lambda eh, bb=blk: eh.tensor_scalar(
                FFNB[:, bb * 1024:(bb + 1) * 1024], FFNB[:, bb * 1024:(bb + 1) * 1024],
                ST2[:, bb, 0:1], RR[:, bb:bb + 1], OP.subtract, OP.mult))
        vb()
        for blk in range(2):
            e("vector", lambda eh, bb=blk, s=buf: eh.tensor_tensor(
                FFNB[:, bb * 1024:(bb + 1) * 1024], FFNB[:, bb * 1024:(bb + 1) * 1024],
                s[:, bb * 1024:(bb + 1) * 1024], OP.add))
        evf = vpad(inc=None)
        xfree[NTILE + i] = evf
        w("gpsimd", "QV", evf)
        for blk in range(2):
            e("gpsimd", lambda eh, bb=b, ttk=t + blk * 128, o=blk: eh.dma_start(
                out=o_h[bb, ttk:ttk + 128, :], in_=FFNB[:, o * 1024:(o + 1) * 1024]),
              inc="OS", k=16)
        os_done[i] = cnt["OS"]

    w("gpsimd", "OS", cnt["OS"])

    with es, nc.Block() as block:
        @block.sync
        def _(sync):
            for f in ops["sync"]:
                f(sync)

        @block.gpsimd
        def _(gpsimd):
            for f in ops["gpsimd"]:
                f(gpsimd)

        @block.vector
        def _(vector):
            for f in ops["vector"]:
                f(vector)

        @block.scalar
        def _(scalar):
            for f in ops["scalar"]:
                f(scalar)

        @block.tensor
        def _(tensor):
            for f in ops["tensor"]:
                f(tensor)
    return nc


def kernel(L0_attn_tokens, L0_attn_norm_w, L0_attn_W_in, L0_attn_b_in,
           L0_attn_W_out, L0_attn_b_out, L0_attn_W_skip, L0_attn_b_skip,
           L0_attn_log_decay, L0_ffn_U1, L0_ffn_V1, L0_ffn_b1, L0_ffn_U2,
           L0_ffn_V2, L0_ffn_b2):
    f = np.float32
    tok = np.ascontiguousarray(np.asarray(L0_attn_tokens, f))
    rep = lambda v: np.ascontiguousarray(np.broadcast_to(np.asarray(v, f), (128, 1024)))
    chunked = lambda wmat, nch, cols: np.ascontiguousarray(
        np.asarray(wmat, f).reshape(nch, 128, cols).transpose(1, 0, 2))
    com = {
        "winT": chunked(np.asarray(L0_attn_W_in, f).T, 8, 256),
        "woutT": chunked(np.asarray(L0_attn_W_out, f).T, 2, 1024),
        "wskipT": chunked(np.asarray(L0_attn_W_skip, f).T, 8, 1024),
        "u1": chunked(L0_ffn_U1, 8, 256),
        "v1": chunked(L0_ffn_V1, 2, 4096),
        "u2": chunked(L0_ffn_U2, 32, 256),
        "v2": chunked(L0_ffn_V2, 2, 1024),
        "nwf": np.ascontiguousarray(np.asarray(L0_attn_norm_w, f).reshape(8, 128).T),
        "bo": rep(L0_attn_b_out),
        "bs": rep(L0_attn_b_skip),
        "b2": rep(L0_ffn_b2),
        "bin": np.ascontiguousarray(np.asarray(L0_attn_b_in, f).reshape(2, 128).T),
        "b1": np.ascontiguousarray(np.asarray(L0_ffn_b1, f).reshape(32, 128).T),
        "ldec": np.ascontiguousarray(np.asarray(L0_attn_log_decay, f).reshape(2, 128).T),
        "eye": np.eye(128, dtype=f),
    }
    in_maps = []
    for c in range(8):
        m = dict(com)
        m["x"] = np.ascontiguousarray(tok[2 * c:2 * c + 2])
        in_maps.append(m)
    nc = build_nc()
    res = run_bass_kernel_spmd(nc, in_maps, core_ids=list(range(8)))
    out = np.concatenate([res.results[c]["o"] for c in range(8)], axis=0)
    return np.ascontiguousarray(out.astype(np.float32))


# version 16 (final)
# speedup vs baseline: 1.1878x; 1.0394x over previous
import numpy as np
from contextlib import ExitStack

import concourse.bass as bass
import concourse.mybir as mybir
from concourse.bass_utils import run_bass_kernel_spmd

F32 = mybir.dt.float32
AF = mybir.ActivationFunctionType
OP = mybir.AluOpType

B_LOC = 2
S = 2048
D = 1024
TOK = 256
NTILE = B_LOC * S // TOK
ENGS = ("sync", "gpsimd", "vector", "scalar", "tensor")


def build_nc():
    nc = bass.Bass("TRN2", target_bir_lowering=False)
    x_h = nc.dram_tensor("x", [B_LOC, S, D], F32, kind="ExternalInput").ap()
    o_h = nc.dram_tensor("o", [B_LOC, S, D], F32, kind="ExternalOutput").ap()
    winT_h = nc.dram_tensor("winT", [128, 8, 256], F32, kind="ExternalInput").ap()
    woutT_h = nc.dram_tensor("woutT", [128, 2, 1024], F32, kind="ExternalInput").ap()
    wskipT_h = nc.dram_tensor("wskipT", [128, 8, 1024], F32, kind="ExternalInput").ap()
    u1_h = nc.dram_tensor("u1", [128, 8, 256], F32, kind="ExternalInput").ap()
    v1_h = nc.dram_tensor("v1", [128, 2, 4096], F32, kind="ExternalInput").ap()
    u2_h = nc.dram_tensor("u2", [128, 32, 256], F32, kind="ExternalInput").ap()
    v2_h = nc.dram_tensor("v2", [128, 2, 1024], F32, kind="ExternalInput").ap()
    nwf_h = nc.dram_tensor("nwf", [128, 8], F32, kind="ExternalInput").ap()
    bo_h = nc.dram_tensor("bo", [128, 1024], F32, kind="ExternalInput").ap()
    bs_h = nc.dram_tensor("bs", [128, 1024], F32, kind="ExternalInput").ap()
    b2_h = nc.dram_tensor("b2", [128, 1024], F32, kind="ExternalInput").ap()
    bin_h = nc.dram_tensor("bin", [128, 2], F32, kind="ExternalInput").ap()
    b1_h = nc.dram_tensor("b1", [128, 32], F32, kind="ExternalInput").ap()
    ldec_h = nc.dram_tensor("ldec", [128, 2], F32, kind="ExternalInput").ap()
    eye_h = nc.dram_tensor("eye", [128, 128], F32, kind="ExternalInput").ap()

    es = ExitStack()
    sb = lambda name, shape: es.enter_context(nc.sbuf_tensor(name, shape, F32))
    WIN = sb("WIN", [128, 8, 256])
    WOUT = sb("WOUT", [128, 2, 1024])
    WSKIP = sb("WSKIP", [128, 8, 1024])
    V1 = sb("V1", [128, 2, 4096])
    U2 = sb("U2", [128, 32, 256])
    V2 = sb("V2", [128, 2, 1024])
    BADD = sb("BADD", [128, 1024])
    B2R = sb("B2R", [128, 1024])
    BIN = sb("BIN", [128, 2])
    B1 = sb("B1", [128, 32])
    LDEC = sb("LDEC", [128, 2])
    NWF = sb("NWF", [128, 8])
    DECS = sb("DECS", [128, 2])
    EYE = sb("EYE", [128, 128])
    U = sb("U", [128, 2, 2 * S])
    XA = sb("XA", [128, 2048])
    SCR = sb("SCR", [128, 2048])
    T0 = sb("T0", [128, 2048])
    ATT = sb("ATT", [128, 2048])
    FFNB = sb("FFNB", [128, 2048])
    XNT = sb("XNT", [128, 4, 256])
    ST6 = sb("ST6", [128, 4, 6])
    ST2 = sb("ST2", [128, 2, 2])
    MSQ = sb("MSQ", [128, 2])
    MS = sb("MS", [128, 2])
    SD = sb("SD", [128, 2])
    RR = sb("RR", [128, 2])
    ZERO = sb("ZERO", [128, 1])
    EPS5 = sb("EPS5", [128, 1])
    EPS4 = sb("EPS4", [128, 1])
    PADT = sb("PADT", [128, 64])
    PADS = sb("PADS", [128, 64])

    ps = lambda name, shape: es.enter_context(nc.psum_tensor(name, shape, F32))
    PT0 = ps("PT0", [128, 4, 128])
    PT1 = ps("PT1", [128, 4, 128])
    PA = ps("PA", [128, 512])
    PB = ps("PB", [128, 512])
    PU = ps("PU", [128, 512])
    PP = ps("PP", [128, 512])
    PH = ps("PH", [128, 2, 256])
    PQ = ps("PQ", [128, 2, 256])

    sem_names = ("LXA", "LXB", "LWC", "LW", "LW2", "QV", "QS", "QT", "OS")
    semh = {s: es.enter_context(nc.semaphore(s)) for s in sem_names}

    ops = {e: [] for e in ENGS}
    cnt = {s: 0 for s in sem_names}

    def w(eng, sem, val):
        if val > 0:
            ops[eng].append(lambda eh, sm=sem, v=val: eh.wait_ge(semh[sm], v))

    AUTO = {"vector": "QV", "scalar": "QS", "tensor": "QT"}

    def e(eng, fn, inc=None, k=1):
        if inc is None and eng in AUTO:
            inc, k = AUTO[eng], 1
        if inc is None:
            ops[eng].append(fn)
            return None
        cnt[inc] += k
        ops[eng].append(lambda eh, f=fn, s=inc, kk=k: f(eh).then_inc(semh[s], kk))
        return cnt[inc]

    def vb():
        ops["vector"].append(lambda eh, v=cnt["QV"]: eh.wait_ge(semh["QV"], v))

    _pv = [0]

    def vpad(inc=None):
        j1 = _pv[0] % 64
        j2 = (_pv[0] + 1) % 64
        _pv[0] += 2
        e("vector", lambda eh, jj=j1: eh.tensor_copy(PADT[:, jj:jj + 1], ZERO[:]))
        return e("vector", lambda eh, jj=j2: eh.tensor_copy(PADT[:, jj:jj + 1], ZERO[:]))

    _ps = [0]

    def spad(inc=None):
        j = _ps[0] % 64
        _ps[0] += 1
        return e("scalar", lambda eh, jj=j: eh.activation(PADS[:, jj:jj + 1], ZERO[:], AF.Copy))

    for name, dst, src in (
        ("eye", EYE[:], eye_h[:]), ("nw", NWF[:], nwf_h[:]), ("bin", BIN[:], bin_h[:]),
        ("b1", B1[:], b1_h[:]), ("ldec", LDEC[:], ldec_h[:]), ("bo", BADD[:], bo_h[:]),
        ("bs", SCR[:, 0:1024], bs_h[:]), ("b2", B2R[:], b2_h[:]),
        ("winT", WIN[:], winT_h[:]),
    ):
        e("sync", lambda eh, d=dst, s=src: eh.dma_start(out=d, in_=s), inc="LWC", k=16)
    for name, dst, src in (("woutT", WOUT[:], woutT_h[:]), ("wskipT", WSKIP[:], wskipT_h[:])):
        e("sync", lambda eh, d=dst, s=src: eh.dma_start(out=d, in_=s), inc="LW", k=16)
    LWC_ALL, LW_ALL = 144, 32
    wcnt = {k: LWC_ALL for k in ("eye", "nw", "bin", "b1", "ldec", "bo", "bs", "b2", "winT")}
    wcnt.update({"woutT": LW_ALL, "wskipT": LW_ALL})
    w2cnt = {}
    for name, dst, src in (("v1", V1[:], v1_h[:]), ("u2", U2[:], u2_h[:]),
                           ("v2", V2[:], v2_h[:])):
        w2cnt[name] = e("scalar", lambda eh, d=dst, s=src: eh.dma_start(out=d, in_=s), inc="LW2", k=16)

    w("scalar", "LWC", wcnt["ldec"])
    ops["scalar"].append(lambda eh: eh.wait_ge(semh["QV"], 4))
    e("scalar", lambda eh: eh.activation(DECS[:], LDEC[:], AF.Sigmoid))
    ss_decs = spad(inc=None)

    e("vector", lambda eh: eh.memset(ZERO[:], 0.0))
    e("vector", lambda eh: eh.memset(EPS5[:], 1e-5))
    e("vector", lambda eh: eh.memset(EPS4[:], 1e-4))
    e("vector", lambda eh: eh.memset(PADT[:], 0.0))
    vb()
    w("vector", "LWC", wcnt["bs"])
    e("vector", lambda eh: eh.tensor_tensor(BADD[:], BADD[:], SCR[:, 0:1024], OP.add))
    vs_badd = vpad(inc=None)
    w("vector", "LWC", wcnt["winT"])
    for c in range(8):
        e("vector", lambda eh, cc=c: eh.tensor_scalar_mul(
            WIN[:, cc, :], WIN[:, cc, :], NWF[:, cc:cc + 1]))
    vs_winfold = vpad(inc=None)

    XB = [XA, SCR]
    xfree = [0] * (2 * NTILE + 2)

    def emit_xload(slot, vs_guard):
        i = slot % NTILE
        b, t = i // 8, (i % 8) * TOK
        buf = XB[slot % 2]
        sem = ("LXA", "LXB")[slot % 2]
        if vs_guard:
            w("gpsimd", "QV", vs_guard)
        for blk in range(2):
            e("gpsimd", lambda eh, bb=b, tt=t + blk * 128, dst=buf[:, blk * 1024:(blk + 1) * 1024]:
              eh.dma_start(out=dst, in_=x_h[bb, tt:tt + 128, :]), inc=sem, k=16)

    def wait_x(eng, slot):
        w(eng, ("LXA", "LXB")[slot % 2], 32 * (slot // 2 + 1))

    def ln_stats(src):
        vb()
        for blk in range(2):
            for g in range(2):
                e("vector", lambda eh, b=blk, gg=g, s=src:
                  eh.bn_stats(ST6[:, 2 * b + gg, :],
                              s[:, b * 1024 + gg * 512: b * 1024 + (gg + 1) * 512]))
        vb()
        for blk in range(2):
            e("vector", lambda eh, b=blk: eh.bn_aggr(ST2[:, b, :], ST6[:, 2 * b:2 * b + 2, :]))

    def sqrt_recip(src_col_fn, eps):
        for blk in range(2):
            e("scalar", lambda eh, b=blk: eh.activation(
                SD[:, b:b + 1], src_col_fn(b), AF.Sqrt, bias=eps[:, 0:1]))
        ss = spad(inc=None)
        w("vector", "QS", ss)
        for blk in range(2):
            e("vector", lambda eh, b=blk: eh.reciprocal(RR[:, b:b + 1], SD[:, b:b + 1]))
        vb()

    def emit_tp_group(src, g):
        for blk in range(2):
            PT = (PT0, PT1)[blk]
            for j in range(4):
                c = 4 * g + j
                e("tensor", lambda eh, bb=blk, cc=c, jj=j, P=PT, s=src: eh.transpose(
                    P[:, jj, :], s[:, bb * 1024 + cc * 128: bb * 1024 + (cc + 1) * 128], EYE[:]))
        w("vector", "QT", cnt["QT"])
        for blk in range(2):
            PT = (PT0, PT1)[blk]
            e("vector", lambda eh, bb=blk, P=PT: eh.tensor_copy(
                XNT[:, :, bb * 128:(bb + 1) * 128], P[:, :, :]))
        ev = vpad(inc=None)
        w("tensor", "QV", ev)

    def vfront(slot, xn_dst, tp_guard_ts=0):
        buf = XB[slot % 2]
        wait_x("vector", slot)
        vs_st = None
        ln_stats(buf)
        vs_st = vpad(inc=None)
        w("scalar", "QV", vs_st)
        sqrt_recip(lambda blk: ST2[:, blk, 1:2], EPS5)
        for blk in range(2):
            e("vector", lambda eh, bb=blk, s=buf: eh.tensor_scalar(
                T0[:, bb * 1024:(bb + 1) * 1024], s[:, bb * 1024:(bb + 1) * 1024],
                ST2[:, bb, 0:1], RR[:, bb:bb + 1], OP.subtract, OP.mult))
        ln_stats(T0)
        vb()
        for blk in range(2):
            e("vector", lambda eh, bb=blk: eh.tensor_tensor(
                MSQ[:, bb:bb + 1], ST2[:, bb, 0:1], ST2[:, bb, 0:1], OP.mult))
        vb()
        for blk in range(2):
            e("vector", lambda eh, bb=blk: eh.tensor_tensor(
                MS[:, bb:bb + 1], MSQ[:, bb:bb + 1], ST2[:, bb, 1:2], OP.add))
        vs_st2 = vpad(inc=None)
        w("scalar", "QV", vs_st2)
        sqrt_recip(lambda blk: MS[:, blk:blk + 1], EPS4)
        if tp_guard_ts:
            w("vector", "QT", tp_guard_ts)
        for blk in range(2):
            e("vector", lambda eh, bb=blk, d=xn_dst: eh.tensor_scalar_mul(
                d[:, bb * 1024:(bb + 1) * 1024], T0[:, bb * 1024:(bb + 1) * 1024],
                RR[:, bb:bb + 1]))
        return vpad(inc=None)

    vs_uevac = [0] * NTILE
    vs_xn1 = [0] * NTILE
    ts_tp1 = [0] * NTILE

    emit_xload(0, 0)
    emit_xload(1, vs_badd)
    xnb = lambda i: (ATT, FFNB)[i % 2]
    vs_xn1[0] = vfront(0, xnb(0))
    xfree[0] = vs_xn1[0]
    emit_xload(2, xfree[0])
    vs_xn1[1] = vfront(1, xnb(1))
    xfree[1] = vs_xn1[1]
    emit_xload(3, xfree[1])

    for i in range(NTILE):
        b, t = i // 8, (i % 8) * TOK
        if i == 0:
            w("tensor", "QV", vs_winfold)
        w("tensor", "QV", vs_xn1[i])
        if i >= 1:
            w("tensor", "QV", vs_uevac[i - 1])
        for g in range(2):
            emit_tp_group(xnb(i), g)
            for nt in range(2):
                PO = (PU, PA)[nt]
                for c in range(4 * g, 4 * g + 4):
                    e("tensor", lambda eh, n=nt, cc=c, P=PO: eh.matmul(
                        P[:, 0:256], WIN[:, cc, n * 128:(n + 1) * 128],
                        XNT[:, cc % 4, :], start=(cc == 0), stop=(cc == 7)))
        ts_tp1[i] = cnt["QT"]
        w("vector", "QT", ts_tp1[i])
        if i == 0:
            w("vector", "LWC", wcnt["bin"])
        for nt in range(2):
            PO = (PU, PA)[nt]
            e("vector", lambda eh, n=nt, bb=b, tt=t, P=PO: eh.tensor_scalar_add(
                U[:, n, bb * S + tt: bb * S + tt + TOK], P[:, 0:256],
                BIN[:, n:n + 1]))
        vs_uevac[i] = vpad(inc=None)
        if i + 2 < NTILE:
            vs_xn1[i + 2] = vfront(i + 2, xnb(i + 2), tp_guard_ts=ts_tp1[i])
            xfree[i + 2] = vs_xn1[i + 2]
            if i + 4 < NTILE:
                emit_xload(i + 4, xfree[i + 2])

    w("vector", "LW", wcnt["wskipT"])
    for c in range(8):
        e("vector", lambda eh, cc=c: eh.tensor_scalar_mul(
            WSKIP[:, cc, :], WSKIP[:, cc, :], NWF[:, cc:cc + 1]))
    vpad()
    w("scalar", "QT", ts_tp1[NTILE - 1])
    w2cnt["u1"] = e("scalar", lambda eh: eh.dma_start(out=WIN[:], in_=u1_h[:]), inc="LW2", k=16)

    w("vector", "QS", ss_decs)
    vb()
    for nt in range(2):
        e("vector", lambda eh: eh.memset(SCR[:], 1.0))
        vb()
        e("vector", lambda eh, n=nt: eh.tensor_scalar_mul(SCR[:], SCR[:], DECS[:, n:n + 1]))
        vb()
        for bb in range(2):
            e("vector", lambda eh, n=nt, b2=bb: eh.tensor_tensor_scan(
                U[:, n, b2 * S:(b2 + 1) * S], SCR[:], U[:, n, b2 * S:(b2 + 1) * S],
                ZERO[:, 0:1], OP.mult, OP.add))
        vb()
    vs_scan = vpad(inc=None)
    w("tensor", "QV", vs_scan)
    emit_xload(NTILE + 0, xfree[14])

    os_done = [0] * NTILE

    def vfront2(i):
        slot = NTILE + i
        buf = XB[i % 2]
        wait_x("vector", slot)
        ln_stats(buf)
        vs_st = vpad(inc=None)
        w("scalar", "QV", vs_st)
        sqrt_recip(lambda blk: ST2[:, blk, 1:2], EPS5)
        for blk in range(2):
            e("vector", lambda eh, bb=blk, s=buf: eh.tensor_scalar(
                T0[:, bb * 1024:(bb + 1) * 1024], s[:, bb * 1024:(bb + 1) * 1024],
                ST2[:, bb, 0:1], RR[:, bb:bb + 1], OP.subtract, OP.mult))
        ln_stats(T0)
        vb()
        for blk in range(2):
            e("vector", lambda eh, bb=blk: eh.tensor_tensor(
                MSQ[:, bb:bb + 1], ST2[:, bb, 0:1], ST2[:, bb, 0:1], OP.mult))
        vb()
        for blk in range(2):
            e("vector", lambda eh, bb=blk: eh.tensor_tensor(
                MS[:, bb:bb + 1], MSQ[:, bb:bb + 1], ST2[:, bb, 1:2], OP.add))
        vs_st2 = vpad(inc=None)
        w("scalar", "QV", vs_st2)
        sqrt_recip(lambda blk: MS[:, blk:blk + 1], EPS4)
        for blk in range(2):
            e("vector", lambda eh, bb=blk: eh.tensor_scalar_mul(
                ATT[:, bb * 1024:(bb + 1) * 1024], T0[:, bb * 1024:(bb + 1) * 1024],
                RR[:, bb:bb + 1]))
        return vpad(inc=None)

    vs_xn2 = [0] * NTILE
    vs_xn2[0] = vfront2(0)
    emit_xload(NTILE + 1, vs_scan)

    COMBO = [(0, 0, PA), (0, 1, PB), (1, 0, PU), (1, 1, PP)]
    for i in range(NTILE):
        b, t = i // 8, (i % 8) * TOK
        buf = XB[i % 2]
        if 1 <= i and i + 1 < NTILE:
            emit_xload(NTILE + i + 1, xfree[NTILE + i - 1])

        w("tensor", "QV", vs_xn2[i])
        for g in range(2):
            emit_tp_group(ATT, g)
            if g == 0:
                for blk, half, PO in COMBO:
                    for nt in range(2):
                        e("tensor", lambda eh, n=nt, bb=b, ttk=t + blk * 128, hh=half, P=PO: eh.matmul(
                            P[:], U[:, n, bb * S + ttk: bb * S + ttk + 128],
                            WOUT[:, n, hh * 512:(hh + 1) * 512],
                            start=(n == 0), stop=False))
            for blk, half, PO in COMBO:
                for c in range(4 * g, 4 * g + 4):
                    e("tensor", lambda eh, cc=c, bb=blk, hh=half, P=PO: eh.matmul(
                        P[:], XNT[:, cc % 4, bb * 128:(bb + 1) * 128],
                        WSKIP[:, cc, hh * 512:(hh + 1) * 512],
                        start=False, stop=(cc == 7)))
        ts_y = cnt["QT"]

        w("vector", "QT", ts_y)
        if i == 0:
            w("vector", "LWC", wcnt["b2"])
        for blk, half, PO in COMBO:
            e("vector", lambda eh, bb=blk, hh=half, P=PO: eh.tensor_tensor(
                ATT[:, bb * 1024 + hh * 512: bb * 1024 + (hh + 1) * 512], P[:],
                T0[:, bb * 1024 + hh * 512: bb * 1024 + (hh + 1) * 512], OP.add))
        vb()
        for blk, half, PO in COMBO:
            e("vector", lambda eh, bb=blk, hh=half: eh.tensor_tensor(
                ATT[:, bb * 1024 + hh * 512: bb * 1024 + (hh + 1) * 512],
                ATT[:, bb * 1024 + hh * 512: bb * 1024 + (hh + 1) * 512],
                BADD[:, hh * 512:(hh + 1) * 512], OP.add))
        ev = vpad(inc=None)
        w("tensor", "QV", ev)
        ln_stats(ATT)
        vs_st = vpad(inc=None)
        w("scalar", "QV", vs_st)
        sqrt_recip(lambda blk: ST2[:, blk, 1:2], EPS5)
        for blk in range(2):
            e("vector", lambda eh, bb=blk: eh.tensor_scalar(
                T0[:, bb * 1024:(bb + 1) * 1024], ATT[:, bb * 1024:(bb + 1) * 1024],
                ST2[:, bb, 0:1], RR[:, bb:bb + 1], OP.subtract, OP.mult))
        vb()
        for blk in range(2):
            e("vector", lambda eh, bb=blk, s=buf: eh.tensor_tensor(
                s[:, bb * 1024:(bb + 1) * 1024], s[:, bb * 1024:(bb + 1) * 1024],
                T0[:, bb * 1024:(bb + 1) * 1024], OP.add))
        ln_stats(buf)
        vs_st = vpad(inc=None)
        w("scalar", "QV", vs_st)
        sqrt_recip(lambda blk: ST2[:, blk, 1:2], EPS5)
        for blk in range(2):
            e("vector", lambda eh, bb=blk, s=buf: eh.tensor_scalar(
                T0[:, bb * 1024:(bb + 1) * 1024], s[:, bb * 1024:(bb + 1) * 1024],
                ST2[:, bb, 0:1], RR[:, bb:bb + 1], OP.subtract, OP.mult))
        vs_t1 = vpad(inc=None)

        w("tensor", "QV", vs_t1)
        if i == 0:
            w("tensor", "LW2", 64)
        for g in range(2):
            emit_tp_group(T0, g)
            for r in range(2):
                PO = (PU, PA)[r]
                for c in range(4 * g, 4 * g + 4):
                    e("tensor", lambda eh, rr=r, cc=c, P=PO: eh.matmul(
                        P[:, 0:256], WIN[:, cc, rr * 128:(rr + 1) * 128],
                        XNT[:, cc % 4, :], start=(cc == 0), stop=(cc == 7)))
        w("vector", "QT", cnt["QT"])
        if i >= 1:
            w("vector", "OS", os_done[i - 1])
        for r in range(2):
            PO = (PU, PA)[r]
            e("vector", lambda eh, rr=r, P=PO: eh.tensor_copy(
                FFNB[:, rr * 256:(rr + 1) * 256], P[:, 0:256]))
        vs_p = vpad(inc=None)
        w("tensor", "QV", vs_p)

        if i + 1 < NTILE:
            vs_xn2[i + 1] = vfront2(i + 1)

        gelu_mark = [0] * 32
        for c in range(33):
            if c < 32:
                if c >= 2:
                    w("tensor", "QS", gelu_mark[c - 2])
                for r in range(2):
                    e("tensor", lambda eh, rr=r, cc=c: eh.matmul(
                        PH[:, cc % 2, :], V1[:, rr, cc * 128:(cc + 1) * 128],
                        FFNB[:, rr * 256:(rr + 1) * 256],
                        start=(rr == 0), stop=(rr == 1)))
                w("scalar", "QT", cnt["QT"])
                if i == 0 and c == 0:
                    w("scalar", "LWC", wcnt["b1"])
                e("scalar", lambda eh, cc=c: eh.activation(
                    FFNB[:, 1536 + (cc % 2) * 256: 1536 + (cc % 2 + 1) * 256],
                    PH[:, cc % 2, :], AF.Gelu_apprx_tanh, bias=B1[:, cc:cc + 1]))
                gelu_mark[c] = spad()
            if c >= 1:
                cc0 = c - 1
                w("tensor", "QS", gelu_mark[cc0])
                for r in range(2):
                    PO = PQ[:, 0, :] if r == 0 else PB[:, 0:256]
                    e("tensor", lambda eh, rr=r, cc=cc0, P=PO: eh.matmul(
                        P, U2[:, cc, rr * 128:(rr + 1) * 128],
                        FFNB[:, 1536 + (cc % 2) * 256: 1536 + (cc % 2 + 1) * 256],
                        start=(cc == 0), stop=(cc == 31)))
        ts_q = cnt["QT"]

        w("vector", "QT", ts_q)
        for r in range(2):
            PO = PQ[:, 0, :] if r == 0 else PB[:, 0:256]
            e("vector", lambda eh, rr=r, P=PO: eh.tensor_copy(
                FFNB[:, 1024 + rr * 256:1024 + (rr + 1) * 256], P))
        vs_q = vpad(inc=None)
        w("tensor", "QV", vs_q)
        for blk in range(2):
            for half in range(2):
                PO = (PA, PB)[half]
                for r in range(2):
                    e("tensor", lambda eh, rr=r, bb=blk, hh=half, P=PO: eh.matmul(
                        P[:], FFNB[:, 1024 + rr * 256 + bb * 128: 1024 + rr * 256 + (bb + 1) * 128],
                        V2[:, rr, hh * 512:(hh + 1) * 512],
                        start=(rr == 0), stop=(rr == 1)))
            w("vector", "QT", cnt["QT"])
            for half in range(2):
                PO = (PA, PB)[half]
                e("vector", lambda eh, bb=blk, hh=half, P=PO: eh.tensor_tensor(
                    FFNB[:, bb * 1024 + hh * 512: bb * 1024 + (hh + 1) * 512], P[:],
                    B2R[:, hh * 512:(hh + 1) * 512], OP.add))
            evb = vpad(inc=None)
            w("tensor", "QV", evb)
        ln_stats(FFNB)
        vs_st = vpad(inc=None)
        w("scalar", "QV", vs_st)
        sqrt_recip(lambda blk: ST2[:, blk, 1:2], EPS5)
        for blk in range(2):
            e("vector", lambda eh, bb=blk: eh.tensor_scalar(
                FFNB[:, bb * 1024:(bb + 1) * 1024], FFNB[:, bb * 1024:(bb + 1) * 1024],
                ST2[:, bb, 0:1], RR[:, bb:bb + 1], OP.subtract, OP.mult))
        vb()
        for blk in range(2):
            e("vector", lambda eh, bb=blk, s=buf: eh.tensor_tensor(
                FFNB[:, bb * 1024:(bb + 1) * 1024], FFNB[:, bb * 1024:(bb + 1) * 1024],
                s[:, bb * 1024:(bb + 1) * 1024], OP.add))
        evf = vpad(inc=None)
        xfree[NTILE + i] = evf
        w("gpsimd", "QV", evf)
        for blk in range(2):
            e("gpsimd", lambda eh, bb=b, ttk=t + blk * 128, o=blk: eh.dma_start(
                out=o_h[bb, ttk:ttk + 128, :], in_=FFNB[:, o * 1024:(o + 1) * 1024]),
              inc="OS", k=16)
        os_done[i] = cnt["OS"]

    w("gpsimd", "OS", cnt["OS"])

    with es, nc.Block() as block:
        @block.sync
        def _(sync):
            for f in ops["sync"]:
                f(sync)

        @block.gpsimd
        def _(gpsimd):
            for f in ops["gpsimd"]:
                f(gpsimd)

        @block.vector
        def _(vector):
            for f in ops["vector"]:
                f(vector)

        @block.scalar
        def _(scalar):
            for f in ops["scalar"]:
                f(scalar)

        @block.tensor
        def _(tensor):
            for f in ops["tensor"]:
                f(tensor)
    return nc


def kernel(L0_attn_tokens, L0_attn_norm_w, L0_attn_W_in, L0_attn_b_in,
           L0_attn_W_out, L0_attn_b_out, L0_attn_W_skip, L0_attn_b_skip,
           L0_attn_log_decay, L0_ffn_U1, L0_ffn_V1, L0_ffn_b1, L0_ffn_U2,
           L0_ffn_V2, L0_ffn_b2):
    f = np.float32
    tok = np.ascontiguousarray(np.asarray(L0_attn_tokens, f))
    rep = lambda v: np.ascontiguousarray(np.broadcast_to(np.asarray(v, f), (128, 1024)))
    chunked = lambda wmat, nch, cols: np.ascontiguousarray(
        np.asarray(wmat, f).reshape(nch, 128, cols).transpose(1, 0, 2))
    com = {
        "winT": chunked(np.asarray(L0_attn_W_in, f).T, 8, 256),
        "woutT": chunked(np.asarray(L0_attn_W_out, f).T, 2, 1024),
        "wskipT": chunked(np.asarray(L0_attn_W_skip, f).T, 8, 1024),
        "u1": chunked(L0_ffn_U1, 8, 256),
        "v1": chunked(L0_ffn_V1, 2, 4096),
        "u2": chunked(L0_ffn_U2, 32, 256),
        "v2": chunked(L0_ffn_V2, 2, 1024),
        "nwf": np.ascontiguousarray(np.asarray(L0_attn_norm_w, f).reshape(8, 128).T),
        "bo": rep(L0_attn_b_out),
        "bs": rep(L0_attn_b_skip),
        "b2": rep(L0_ffn_b2),
        "bin": np.ascontiguousarray(np.asarray(L0_attn_b_in, f).reshape(2, 128).T),
        "b1": np.ascontiguousarray(np.asarray(L0_ffn_b1, f).reshape(32, 128).T),
        "ldec": np.ascontiguousarray(np.asarray(L0_attn_log_decay, f).reshape(2, 128).T),
        "eye": np.eye(128, dtype=f),
    }
    in_maps = []
    for c in range(8):
        m = dict(com)
        m["x"] = np.ascontiguousarray(tok[2 * c:2 * c + 2])
        in_maps.append(m)
    nc = build_nc()
    res = run_bass_kernel_spmd(nc, in_maps, core_ids=list(range(8)))
    out = np.concatenate([res.results[c]["o"] for c in range(8)], axis=0)
    return np.ascontiguousarray(out.astype(np.float32))


# version 17
# speedup vs baseline: 1.2383x; 1.0425x over previous
import numpy as np
from contextlib import ExitStack

import concourse.bass as bass
import concourse.mybir as mybir
from concourse.bass_utils import run_bass_kernel_spmd

F32 = mybir.dt.float32
AF = mybir.ActivationFunctionType
OP = mybir.AluOpType

B_LOC = 2
S = 2048
D = 1024
TOK = 256
NTILE = B_LOC * S // TOK
ENGS = ("sync", "gpsimd", "vector", "scalar", "tensor")


def build_nc():
    nc = bass.Bass("TRN2", target_bir_lowering=False)
    x_h = nc.dram_tensor("x", [B_LOC, S, D], F32, kind="ExternalInput").ap()
    o_h = nc.dram_tensor("o", [B_LOC, S, D], F32, kind="ExternalOutput").ap()
    winT_h = nc.dram_tensor("winT", [128, 8, 256], F32, kind="ExternalInput").ap()
    woutT_h = nc.dram_tensor("woutT", [128, 2, 1024], F32, kind="ExternalInput").ap()
    wskipT_h = nc.dram_tensor("wskipT", [128, 8, 1024], F32, kind="ExternalInput").ap()
    u1_h = nc.dram_tensor("u1", [128, 8, 256], F32, kind="ExternalInput").ap()
    v1_h = nc.dram_tensor("v1", [128, 2, 4096], F32, kind="ExternalInput").ap()
    u2_h = nc.dram_tensor("u2", [128, 32, 256], F32, kind="ExternalInput").ap()
    v2_h = nc.dram_tensor("v2", [128, 2, 1024], F32, kind="ExternalInput").ap()
    nwf_h = nc.dram_tensor("nwf", [128, 8], F32, kind="ExternalInput").ap()
    bo_h = nc.dram_tensor("bo", [128, 1024], F32, kind="ExternalInput").ap()
    bs_h = nc.dram_tensor("bs", [128, 1024], F32, kind="ExternalInput").ap()
    b2_h = nc.dram_tensor("b2", [128, 1024], F32, kind="ExternalInput").ap()
    bin_h = nc.dram_tensor("bin", [128, 2], F32, kind="ExternalInput").ap()
    b1_h = nc.dram_tensor("b1", [128, 32], F32, kind="ExternalInput").ap()
    ldec_h = nc.dram_tensor("ldec", [128, 2], F32, kind="ExternalInput").ap()
    eye_h = nc.dram_tensor("eye", [128, 128], F32, kind="ExternalInput").ap()

    es = ExitStack()
    sb = lambda name, shape: es.enter_context(nc.sbuf_tensor(name, shape, F32))
    WIN = sb("WIN", [128, 8, 256])
    WOUT = sb("WOUT", [128, 2, 1024])
    WSKIP = sb("WSKIP", [128, 8, 1024])
    V1 = sb("V1", [128, 2, 4096])
    U2 = sb("U2", [128, 32, 256])
    V2 = sb("V2", [128, 2, 1024])
    BADD = sb("BADD", [128, 1024])
    B2R = sb("B2R", [128, 1024])
    BIN = sb("BIN", [128, 2])
    B1 = sb("B1", [128, 32])
    LDEC = sb("LDEC", [128, 2])
    NWF = sb("NWF", [128, 8])
    DECS = sb("DECS", [128, 2])
    EYE = sb("EYE", [128, 128])
    U = sb("U", [128, 2, 2 * S])
    XA = sb("XA", [128, 2048])
    SCR = sb("SCR", [128, 2048])
    T0 = sb("T0", [128, 2048])
    ATT = sb("ATT", [128, 2048])
    FFNB = sb("FFNB", [128, 2048])
    XNT = sb("XNT", [128, 4, 256])
    ST6 = sb("ST6", [128, 4, 6])
    ST2 = sb("ST2", [128, 2, 2])
    MSQ = sb("MSQ", [128, 2])
    MS = sb("MS", [128, 2])
    SD = sb("SD", [128, 2])
    RR = sb("RR", [128, 2])
    SD2 = sb("SD2", [128, 2])
    RT = sb("RT", [128, 2])
    R2 = sb("R2", [128, 2])
    C1E4 = sb("C1E4", [128, 1])
    E4E5 = sb("E4E5", [128, 1])
    ZERO = sb("ZERO", [128, 1])
    EPS5 = sb("EPS5", [128, 1])
    EPS4 = sb("EPS4", [128, 1])
    PADT = sb("PADT", [128, 64])
    PADS = sb("PADS", [128, 64])

    ps = lambda name, shape: es.enter_context(nc.psum_tensor(name, shape, F32))
    PT0 = ps("PT0", [128, 4, 128])
    PT1 = ps("PT1", [128, 4, 128])
    PA = ps("PA", [128, 512])
    PB = ps("PB", [128, 512])
    PU = ps("PU", [128, 512])
    PP = ps("PP", [128, 512])
    PH = ps("PH", [128, 2, 256])
    PQ = ps("PQ", [128, 2, 256])

    sem_names = ("LXA", "LXB", "LWC", "LW", "LW2", "QV", "QS", "QT", "OS")
    semh = {s: es.enter_context(nc.semaphore(s)) for s in sem_names}

    ops = {e: [] for e in ENGS}
    cnt = {s: 0 for s in sem_names}

    def w(eng, sem, val):
        if val > 0:
            ops[eng].append(lambda eh, sm=sem, v=val: eh.wait_ge(semh[sm], v))

    AUTO = {"vector": "QV", "scalar": "QS", "tensor": "QT"}

    def e(eng, fn, inc=None, k=1):
        if inc is None and eng in AUTO:
            inc, k = AUTO[eng], 1
        if inc is None:
            ops[eng].append(fn)
            return None
        cnt[inc] += k
        ops[eng].append(lambda eh, f=fn, s=inc, kk=k: f(eh).then_inc(semh[s], kk))
        return cnt[inc]

    def vb():
        ops["vector"].append(lambda eh, v=cnt["QV"]: eh.wait_ge(semh["QV"], v))

    _pv = [0]

    def vpad(inc=None):
        j1 = _pv[0] % 64
        j2 = (_pv[0] + 1) % 64
        _pv[0] += 2
        e("vector", lambda eh, jj=j1: eh.tensor_copy(PADT[:, jj:jj + 1], ZERO[:]))
        return e("vector", lambda eh, jj=j2: eh.tensor_copy(PADT[:, jj:jj + 1], ZERO[:]))

    _ps = [0]

    def spad(inc=None):
        j = _ps[0] % 64
        _ps[0] += 1
        return e("scalar", lambda eh, jj=j: eh.activation(PADS[:, jj:jj + 1], ZERO[:], AF.Copy))

    for name, dst, src in (
        ("eye", EYE[:], eye_h[:]), ("nw", NWF[:], nwf_h[:]), ("bin", BIN[:], bin_h[:]),
        ("b1", B1[:], b1_h[:]), ("ldec", LDEC[:], ldec_h[:]), ("bo", BADD[:], bo_h[:]),
        ("bs", SCR[:, 0:1024], bs_h[:]), ("b2", B2R[:], b2_h[:]),
        ("winT", WIN[:], winT_h[:]),
    ):
        e("sync", lambda eh, d=dst, s=src: eh.dma_start(out=d, in_=s), inc="LWC", k=16)
    for name, dst, src in (("woutT", WOUT[:], woutT_h[:]), ("wskipT", WSKIP[:], wskipT_h[:])):
        e("sync", lambda eh, d=dst, s=src: eh.dma_start(out=d, in_=s), inc="LW", k=16)
    LWC_ALL, LW_ALL = 144, 32
    wcnt = {k: LWC_ALL for k in ("eye", "nw", "bin", "b1", "ldec", "bo", "bs", "b2", "winT")}
    wcnt.update({"woutT": LW_ALL, "wskipT": LW_ALL})
    w2cnt = {}
    for name, dst, src in (("v1", V1[:], v1_h[:]), ("u2", U2[:], u2_h[:]),
                           ("v2", V2[:], v2_h[:])):
        w2cnt[name] = e("scalar", lambda eh, d=dst, s=src: eh.dma_start(out=d, in_=s), inc="LW2", k=16)

    w("scalar", "LWC", wcnt["ldec"])
    ops["scalar"].append(lambda eh: eh.wait_ge(semh["QV"], 4))
    e("scalar", lambda eh: eh.activation(DECS[:], LDEC[:], AF.Sigmoid))
    ss_decs = spad(inc=None)

    e("vector", lambda eh: eh.memset(ZERO[:], 0.0))
    e("vector", lambda eh: eh.memset(EPS5[:], 1e-5))
    e("vector", lambda eh: eh.memset(EPS4[:], 1e-4))
    e("vector", lambda eh: eh.memset(C1E4[:], 1.0001))
    e("vector", lambda eh: eh.memset(E4E5[:], 1.0000000000000002e-09))
    e("vector", lambda eh: eh.memset(PADT[:], 0.0))
    vb()
    w("vector", "LWC", wcnt["bs"])
    e("vector", lambda eh: eh.tensor_tensor(BADD[:], BADD[:], SCR[:, 0:1024], OP.add))
    vs_badd = vpad(inc=None)
    w("vector", "LWC", wcnt["winT"])
    for c in range(8):
        e("vector", lambda eh, cc=c: eh.tensor_scalar_mul(
            WIN[:, cc, :], WIN[:, cc, :], NWF[:, cc:cc + 1]))
    vs_winfold = vpad(inc=None)

    XB = [XA, SCR]
    xfree = [0] * (2 * NTILE + 2)

    def emit_xload(slot, vs_guard):
        i = slot % NTILE
        b, t = i // 8, (i % 8) * TOK
        buf = XB[slot % 2]
        sem = ("LXA", "LXB")[slot % 2]
        if vs_guard:
            w("gpsimd", "QV", vs_guard)
        for blk in range(2):
            e("gpsimd", lambda eh, bb=b, tt=t + blk * 128, dst=buf[:, blk * 1024:(blk + 1) * 1024]:
              eh.dma_start(out=dst, in_=x_h[bb, tt:tt + 128, :]), inc=sem, k=16)

    def wait_x(eng, slot):
        w(eng, ("LXA", "LXB")[slot % 2], 32 * (slot // 2 + 1))

    def ln_stats(src):
        vb()
        for blk in range(2):
            for g in range(2):
                e("vector", lambda eh, b=blk, gg=g, s=src:
                  eh.bn_stats(ST6[:, 2 * b + gg, :],
                              s[:, b * 1024 + gg * 512: b * 1024 + (gg + 1) * 512]))
        vb()
        for blk in range(2):
            e("vector", lambda eh, b=blk: eh.bn_aggr(ST2[:, b, :], ST6[:, 2 * b:2 * b + 2, :]))

    def sqrt_recip(src_col_fn, eps):
        for blk in range(2):
            e("scalar", lambda eh, b=blk: eh.activation(
                SD[:, b:b + 1], src_col_fn(b), AF.Sqrt, bias=eps[:, 0:1]))
        ss = spad(inc=None)
        w("vector", "QS", ss)
        for blk in range(2):
            e("vector", lambda eh, b=blk: eh.reciprocal(RR[:, b:b + 1], SD[:, b:b + 1]))
        vb()

    def emit_tp_group(src, g):
        for blk in range(2):
            PT = (PT0, PT1)[blk]
            for j in range(4):
                c = 4 * g + j
                e("tensor", lambda eh, bb=blk, cc=c, jj=j, P=PT, s=src: eh.transpose(
                    P[:, jj, :], s[:, bb * 1024 + cc * 128: bb * 1024 + (cc + 1) * 128], EYE[:]))
        w("vector", "QT", cnt["QT"])
        for blk in range(2):
            PT = (PT0, PT1)[blk]
            e("vector", lambda eh, bb=blk, P=PT: eh.tensor_copy(
                XNT[:, :, bb * 128:(bb + 1) * 128], P[:, :, :]))
        ev = vpad(inc=None)
        w("tensor", "QV", ev)

    def vfront(slot, xn_dst, tp_guard_ts=0):
        buf = XB[slot % 2]
        wait_x("vector", slot)
        vs_st = None
        ln_stats(buf)
        vs_st = vpad(inc=None)
        w("scalar", "QV", vs_st)
        for blk in range(2):
            e("scalar", lambda eh, b=blk: eh.activation(
                SD[:, b:b + 1], ST2[:, b, 1:2], AF.Sqrt, bias=EPS5[:, 0:1]))
        for blk in range(2):
            e("scalar", lambda eh, b=blk: eh.activation(
                SD2[:, b:b + 1], ST2[:, b, 1:2], AF.Sqrt, bias=E4E5[:, 0:1], scale=C1E4[:, 0:1]))
        ssf = spad(inc=None)
        w("vector", "QS", ssf)
        for blk in range(2):
            e("vector", lambda eh, b=blk: eh.reciprocal(RR[:, b:b + 1], SD[:, b:b + 1]))
        for blk in range(2):
            e("vector", lambda eh, b=blk: eh.reciprocal(RT[:, b:b + 1], SD2[:, b:b + 1]))
        vb()
        for blk in range(2):
            e("vector", lambda eh, bb=blk, s=buf: eh.tensor_scalar(
                T0[:, bb * 1024:(bb + 1) * 1024], s[:, bb * 1024:(bb + 1) * 1024],
                ST2[:, bb, 0:1], RR[:, bb:bb + 1], OP.subtract, OP.mult))
        for blk in range(2):
            e("vector", lambda eh, b=blk: eh.tensor_tensor(
                R2[:, b:b + 1], SD[:, b:b + 1], RT[:, b:b + 1], OP.mult))
        vb()
        if tp_guard_ts:
            w("vector", "QT", tp_guard_ts)
        for blk in range(2):
            e("vector", lambda eh, bb=blk, d=xn_dst: eh.tensor_scalar_mul(
                d[:, bb * 1024:(bb + 1) * 1024], T0[:, bb * 1024:(bb + 1) * 1024],
                R2[:, bb:bb + 1]))
        return vpad(inc=None)

    vs_uevac = [0] * NTILE
    vs_xn1 = [0] * NTILE
    ts_tp1 = [0] * NTILE

    emit_xload(0, 0)
    emit_xload(1, vs_badd)
    xnb = lambda i: (ATT, FFNB)[i % 2]
    vs_xn1[0] = vfront(0, xnb(0))
    xfree[0] = vs_xn1[0]
    emit_xload(2, xfree[0])
    vs_xn1[1] = vfront(1, xnb(1))
    xfree[1] = vs_xn1[1]
    emit_xload(3, xfree[1])

    for i in range(NTILE):
        b, t = i // 8, (i % 8) * TOK
        if i == 0:
            w("tensor", "QV", vs_winfold)
        w("tensor", "QV", vs_xn1[i])
        if i >= 1:
            w("tensor", "QV", vs_uevac[i - 1])
        for g in range(2):
            emit_tp_group(xnb(i), g)
            for nt in range(2):
                PO = (PU, PA)[nt]
                for c in range(4 * g, 4 * g + 4):
                    e("tensor", lambda eh, n=nt, cc=c, P=PO: eh.matmul(
                        P[:, 0:256], WIN[:, cc, n * 128:(n + 1) * 128],
                        XNT[:, cc % 4, :], start=(cc == 0), stop=(cc == 7)))
        ts_tp1[i] = cnt["QT"]
        w("vector", "QT", ts_tp1[i])
        if i == 0:
            w("vector", "LWC", wcnt["bin"])
        for nt in range(2):
            PO = (PU, PA)[nt]
            e("vector", lambda eh, n=nt, bb=b, tt=t, P=PO: eh.tensor_scalar_add(
                U[:, n, bb * S + tt: bb * S + tt + TOK], P[:, 0:256],
                BIN[:, n:n + 1]))
        vs_uevac[i] = vpad(inc=None)
        if i + 2 < NTILE:
            vs_xn1[i + 2] = vfront(i + 2, xnb(i + 2), tp_guard_ts=ts_tp1[i])
            xfree[i + 2] = vs_xn1[i + 2]
            if i + 4 < NTILE:
                emit_xload(i + 4, xfree[i + 2])

    w("vector", "LW", wcnt["wskipT"])
    for c in range(8):
        e("vector", lambda eh, cc=c: eh.tensor_scalar_mul(
            WSKIP[:, cc, :], WSKIP[:, cc, :], NWF[:, cc:cc + 1]))
    vpad()
    w("scalar", "QT", ts_tp1[NTILE - 1])
    w2cnt["u1"] = e("scalar", lambda eh: eh.dma_start(out=WIN[:], in_=u1_h[:]), inc="LW2", k=16)

    w("vector", "QS", ss_decs)
    vb()
    for nt in range(2):
        e("vector", lambda eh: eh.memset(SCR[:], 1.0))
        vb()
        e("vector", lambda eh, n=nt: eh.tensor_scalar_mul(SCR[:], SCR[:], DECS[:, n:n + 1]))
        vb()
        for bb in range(2):
            e("vector", lambda eh, n=nt, b2=bb: eh.tensor_tensor_scan(
                U[:, n, b2 * S:(b2 + 1) * S], SCR[:], U[:, n, b2 * S:(b2 + 1) * S],
                ZERO[:, 0:1], OP.mult, OP.add))
        vb()
    vs_scan = vpad(inc=None)
    w("tensor", "QV", vs_scan)
    emit_xload(NTILE + 0, xfree[14])

    os_done = [0] * NTILE

    def vfront2(i):
        slot = NTILE + i
        buf = XB[i % 2]
        wait_x("vector", slot)
        ln_stats(buf)
        vs_st = vpad(inc=None)
        w("scalar", "QV", vs_st)
        for blk in range(2):
            e("scalar", lambda eh, b=blk: eh.activation(
                SD[:, b:b + 1], ST2[:, b, 1:2], AF.Sqrt, bias=EPS5[:, 0:1]))
        for blk in range(2):
            e("scalar", lambda eh, b=blk: eh.activation(
                SD2[:, b:b + 1], ST2[:, b, 1:2], AF.Sqrt, bias=E4E5[:, 0:1], scale=C1E4[:, 0:1]))
        ssf = spad(inc=None)
        w("vector", "QS", ssf)
        for blk in range(2):
            e("vector", lambda eh, b=blk: eh.reciprocal(RR[:, b:b + 1], SD[:, b:b + 1]))
        for blk in range(2):
            e("vector", lambda eh, b=blk: eh.reciprocal(RT[:, b:b + 1], SD2[:, b:b + 1]))
        vb()
        for blk in range(2):
            e("vector", lambda eh, bb=blk, s=buf: eh.tensor_scalar(
                T0[:, bb * 1024:(bb + 1) * 1024], s[:, bb * 1024:(bb + 1) * 1024],
                ST2[:, bb, 0:1], RR[:, bb:bb + 1], OP.subtract, OP.mult))
        for blk in range(2):
            e("vector", lambda eh, b=blk: eh.tensor_tensor(
                R2[:, b:b + 1], SD[:, b:b + 1], RT[:, b:b + 1], OP.mult))
        vb()
        for blk in range(2):
            e("vector", lambda eh, bb=blk: eh.tensor_scalar_mul(
                ATT[:, bb * 1024:(bb + 1) * 1024], T0[:, bb * 1024:(bb + 1) * 1024],
                R2[:, bb:bb + 1]))
        return vpad(inc=None)

    vs_xn2 = [0] * NTILE
    vs_xn2[0] = vfront2(0)
    emit_xload(NTILE + 1, vs_scan)

    COMBO = [(0, 0, PA), (0, 1, PB), (1, 0, PU), (1, 1, PP)]
    for i in range(NTILE):
        b, t = i // 8, (i % 8) * TOK
        buf = XB[i % 2]
        if 1 <= i and i + 1 < NTILE:
            emit_xload(NTILE + i + 1, xfree[NTILE + i - 1])

        w("tensor", "QV", vs_xn2[i])
        for g in range(2):
            emit_tp_group(ATT, g)
            if g == 0:
                for blk, half, PO in COMBO:
                    for nt in range(2):
                        e("tensor", lambda eh, n=nt, bb=b, ttk=t + blk * 128, hh=half, P=PO: eh.matmul(
                            P[:], U[:, n, bb * S + ttk: bb * S + ttk + 128],
                            WOUT[:, n, hh * 512:(hh + 1) * 512],
                            start=(n == 0), stop=False))
            for blk, half, PO in COMBO:
                for c in range(4 * g, 4 * g + 4):
                    e("tensor", lambda eh, cc=c, bb=blk, hh=half, P=PO: eh.matmul(
                        P[:], XNT[:, cc % 4, bb * 128:(bb + 1) * 128],
                        WSKIP[:, cc, hh * 512:(hh + 1) * 512],
                        start=False, stop=(cc == 7)))
        ts_y = cnt["QT"]

        w("vector", "QT", ts_y)
        if i == 0:
            w("vector", "LWC", wcnt["b2"])
        for blk, half, PO in COMBO:
            e("vector", lambda eh, bb=blk, hh=half, P=PO: eh.tensor_tensor(
                ATT[:, bb * 1024 + hh * 512: bb * 1024 + (hh + 1) * 512], P[:],
                T0[:, bb * 1024 + hh * 512: bb * 1024 + (hh + 1) * 512], OP.add))
        vb()
        for blk, half, PO in COMBO:
            e("vector", lambda eh, bb=blk, hh=half: eh.tensor_tensor(
                ATT[:, bb * 1024 + hh * 512: bb * 1024 + (hh + 1) * 512],
                ATT[:, bb * 1024 + hh * 512: bb * 1024 + (hh + 1) * 512],
                BADD[:, hh * 512:(hh + 1) * 512], OP.add))
        ev = vpad(inc=None)
        w("tensor", "QV", ev)
        ln_stats(ATT)
        vs_st = vpad(inc=None)
        w("scalar", "QV", vs_st)
        sqrt_recip(lambda blk: ST2[:, blk, 1:2], EPS5)
        for blk in range(2):
            e("vector", lambda eh, bb=blk: eh.tensor_scalar(
                T0[:, bb * 1024:(bb + 1) * 1024], ATT[:, bb * 1024:(bb + 1) * 1024],
                ST2[:, bb, 0:1], RR[:, bb:bb + 1], OP.subtract, OP.mult))
        vb()
        for blk in range(2):
            e("vector", lambda eh, bb=blk, s=buf: eh.tensor_tensor(
                s[:, bb * 1024:(bb + 1) * 1024], s[:, bb * 1024:(bb + 1) * 1024],
                T0[:, bb * 1024:(bb + 1) * 1024], OP.add))
        ln_stats(buf)
        vs_st = vpad(inc=None)
        w("scalar", "QV", vs_st)
        sqrt_recip(lambda blk: ST2[:, blk, 1:2], EPS5)
        for blk in range(2):
            e("vector", lambda eh, bb=blk, s=buf: eh.tensor_scalar(
                T0[:, bb * 1024:(bb + 1) * 1024], s[:, bb * 1024:(bb + 1) * 1024],
                ST2[:, bb, 0:1], RR[:, bb:bb + 1], OP.subtract, OP.mult))
        vs_t1 = vpad(inc=None)

        w("tensor", "QV", vs_t1)
        if i == 0:
            w("tensor", "LW2", 64)
        for g in range(2):
            emit_tp_group(T0, g)
            for r in range(2):
                PO = (PU, PA)[r]
                for c in range(4 * g, 4 * g + 4):
                    e("tensor", lambda eh, rr=r, cc=c, P=PO: eh.matmul(
                        P[:, 0:256], WIN[:, cc, rr * 128:(rr + 1) * 128],
                        XNT[:, cc % 4, :], start=(cc == 0), stop=(cc == 7)))
        w("vector", "QT", cnt["QT"])
        if i >= 1:
            w("vector", "OS", os_done[i - 1])
        for r in range(2):
            PO = (PU, PA)[r]
            e("vector", lambda eh, rr=r, P=PO: eh.tensor_copy(
                FFNB[:, rr * 256:(rr + 1) * 256], P[:, 0:256]))
        vs_p = vpad(inc=None)
        w("tensor", "QV", vs_p)

        if i + 1 < NTILE:
            vs_xn2[i + 1] = vfront2(i + 1)

        gelu_mark = [0] * 32
        for c in range(33):
            if c < 32:
                if c >= 2:
                    w("tensor", "QS", gelu_mark[c - 2])
                for r in range(2):
                    e("tensor", lambda eh, rr=r, cc=c: eh.matmul(
                        PH[:, cc % 2, :], V1[:, rr, cc * 128:(cc + 1) * 128],
                        FFNB[:, rr * 256:(rr + 1) * 256],
                        start=(rr == 0), stop=(rr == 1)))
                w("scalar", "QT", cnt["QT"])
                if i == 0 and c == 0:
                    w("scalar", "LWC", wcnt["b1"])
                e("scalar", lambda eh, cc=c: eh.activation(
                    FFNB[:, 1536 + (cc % 2) * 256: 1536 + (cc % 2 + 1) * 256],
                    PH[:, cc % 2, :], AF.Gelu_apprx_tanh, bias=B1[:, cc:cc + 1]))
                gelu_mark[c] = spad()
            if c >= 1:
                cc0 = c - 1
                w("tensor", "QS", gelu_mark[cc0])
                for r in range(2):
                    PO = PQ[:, 0, :] if r == 0 else PB[:, 0:256]
                    e("tensor", lambda eh, rr=r, cc=cc0, P=PO: eh.matmul(
                        P, U2[:, cc, rr * 128:(rr + 1) * 128],
                        FFNB[:, 1536 + (cc % 2) * 256: 1536 + (cc % 2 + 1) * 256],
                        start=(cc == 0), stop=(cc == 31)))
        ts_q = cnt["QT"]

        w("vector", "QT", ts_q)
        for r in range(2):
            PO = PQ[:, 0, :] if r == 0 else PB[:, 0:256]
            e("vector", lambda eh, rr=r, P=PO: eh.tensor_copy(
                FFNB[:, 1024 + rr * 256:1024 + (rr + 1) * 256], P))
        vs_q = vpad(inc=None)
        w("tensor", "QV", vs_q)
        for blk in range(2):
            for half in range(2):
                PO = (PA, PB)[half]
                for r in range(2):
                    e("tensor", lambda eh, rr=r, bb=blk, hh=half, P=PO: eh.matmul(
                        P[:], FFNB[:, 1024 + rr * 256 + bb * 128: 1024 + rr * 256 + (bb + 1) * 128],
                        V2[:, rr, hh * 512:(hh + 1) * 512],
                        start=(rr == 0), stop=(rr == 1)))
            w("vector", "QT", cnt["QT"])
            for half in range(2):
                PO = (PA, PB)[half]
                e("vector", lambda eh, bb=blk, hh=half, P=PO: eh.tensor_tensor(
                    FFNB[:, bb * 1024 + hh * 512: bb * 1024 + (hh + 1) * 512], P[:],
                    B2R[:, hh * 512:(hh + 1) * 512], OP.add))
            evb = vpad(inc=None)
            w("tensor", "QV", evb)
        ln_stats(FFNB)
        vs_st = vpad(inc=None)
        w("scalar", "QV", vs_st)
        sqrt_recip(lambda blk: ST2[:, blk, 1:2], EPS5)
        for blk in range(2):
            e("vector", lambda eh, bb=blk: eh.tensor_scalar(
                FFNB[:, bb * 1024:(bb + 1) * 1024], FFNB[:, bb * 1024:(bb + 1) * 1024],
                ST2[:, bb, 0:1], RR[:, bb:bb + 1], OP.subtract, OP.mult))
        vb()
        for blk in range(2):
            e("vector", lambda eh, bb=blk, s=buf: eh.tensor_tensor(
                FFNB[:, bb * 1024:(bb + 1) * 1024], FFNB[:, bb * 1024:(bb + 1) * 1024],
                s[:, bb * 1024:(bb + 1) * 1024], OP.add))
        evf = vpad(inc=None)
        xfree[NTILE + i] = evf
        w("gpsimd", "QV", evf)
        for blk in range(2):
            e("gpsimd", lambda eh, bb=b, ttk=t + blk * 128, o=blk: eh.dma_start(
                out=o_h[bb, ttk:ttk + 128, :], in_=FFNB[:, o * 1024:(o + 1) * 1024]),
              inc="OS", k=16)
        os_done[i] = cnt["OS"]

    w("gpsimd", "OS", cnt["OS"])

    with es, nc.Block() as block:
        @block.sync
        def _(sync):
            for f in ops["sync"]:
                f(sync)

        @block.gpsimd
        def _(gpsimd):
            for f in ops["gpsimd"]:
                f(gpsimd)

        @block.vector
        def _(vector):
            for f in ops["vector"]:
                f(vector)

        @block.scalar
        def _(scalar):
            for f in ops["scalar"]:
                f(scalar)

        @block.tensor
        def _(tensor):
            for f in ops["tensor"]:
                f(tensor)
    return nc


def kernel(L0_attn_tokens, L0_attn_norm_w, L0_attn_W_in, L0_attn_b_in,
           L0_attn_W_out, L0_attn_b_out, L0_attn_W_skip, L0_attn_b_skip,
           L0_attn_log_decay, L0_ffn_U1, L0_ffn_V1, L0_ffn_b1, L0_ffn_U2,
           L0_ffn_V2, L0_ffn_b2):
    f = np.float32
    tok = np.ascontiguousarray(np.asarray(L0_attn_tokens, f))
    rep = lambda v: np.ascontiguousarray(np.broadcast_to(np.asarray(v, f), (128, 1024)))
    chunked = lambda wmat, nch, cols: np.ascontiguousarray(
        np.asarray(wmat, f).reshape(nch, 128, cols).transpose(1, 0, 2))
    com = {
        "winT": chunked(np.asarray(L0_attn_W_in, f).T, 8, 256),
        "woutT": chunked(np.asarray(L0_attn_W_out, f).T, 2, 1024),
        "wskipT": chunked(np.asarray(L0_attn_W_skip, f).T, 8, 1024),
        "u1": chunked(L0_ffn_U1, 8, 256),
        "v1": chunked(L0_ffn_V1, 2, 4096),
        "u2": chunked(L0_ffn_U2, 32, 256),
        "v2": chunked(L0_ffn_V2, 2, 1024),
        "nwf": np.ascontiguousarray(np.asarray(L0_attn_norm_w, f).reshape(8, 128).T),
        "bo": rep(L0_attn_b_out),
        "bs": rep(L0_attn_b_skip),
        "b2": rep(L0_ffn_b2),
        "bin": np.ascontiguousarray(np.asarray(L0_attn_b_in, f).reshape(2, 128).T),
        "b1": np.ascontiguousarray(np.asarray(L0_ffn_b1, f).reshape(32, 128).T),
        "ldec": np.ascontiguousarray(np.asarray(L0_attn_log_decay, f).reshape(2, 128).T),
        "eye": np.eye(128, dtype=f),
    }
    in_maps = []
    for c in range(8):
        m = dict(com)
        m["x"] = np.ascontiguousarray(tok[2 * c:2 * c + 2])
        in_maps.append(m)
    nc = build_nc()
    res = run_bass_kernel_spmd(nc, in_maps, core_ids=list(range(8)))
    out = np.concatenate([res.results[c]["o"] for c in range(8)], axis=0)
    return np.ascontiguousarray(out.astype(np.float32))
